# version 14
# baseline (speedup 1.0000x reference)
import contextlib
import numpy as np
import ml_dtypes
import concourse.bass as bass
import concourse.mybir as mybir
from concourse.bass_utils import run_bass_kernel_spmd

F32 = mybir.dt.float32
BF16 = mybir.dt.bfloat16
AF = mybir.ActivationFunctionType
ALU = mybir.AluOpType
AX = mybir.AxisListType
BF = ml_dtypes.bfloat16

D = 1024
S = 2048
NKT = 8
NTT = 16
DH = 64
DIM_INNER = 2730
PAD_INNER = 2816
NJ = 22
DEPTH6 = 6
E_GRU = 2048
NET = 16
NEG = -1.0e38

ENGINES = ("pe", "dve", "act", "pool", "sp")
ESEM = dict(pe="pe", dve="dve", act="act", pool="pool", sp="sp_")
NQ = 6
QSEMS = dict(pool=[f"wq{i}" for i in range(NQ)],
             sp=[f"xq{i}" for i in range(NQ)])
ALL_SEMS = ["pe", "dve", "act", "pool"] + QSEMS["pool"] + QSEMS["sp"]


class PL:

    def __init__(self, sems):
        self.phase = ""
        self.labels = {e: [] for e in ENGINES}
        self.ops = {e: [] for e in ENGINES}
        self.sems = sems
        self.cnt = {s: 0 for s in sems}
        self.hwm = {e: {s: 0 for s in sems} for e in ENGINES}
        self.pending = {e: [] for e in ENGINES}
        self.qrot = {q: 0 for q in QSEMS}

    def fence(self, gate_pe=False):
        toks = [self.now(s) for s in ALL_SEMS if s != "pool"]
        for e in ENGINES:
            if e == "pe" and not gate_pe:
                continue
            self.pending[e] = list(toks)

    def op(self, eng, fn, deps=(), tick=False, dma=False, force=()):
        own = ESEM[eng]
        qsems = QSEMS.get(eng, ())
        waits = []
        force = list(force)
        tname, amt = (None, 0)
        if dma:
            tname = qsems[self.qrot[eng] % NQ]
            self.qrot[eng] += 1
            amt = 16
            if self.cnt[tname] > 0:
                force.append((tname, self.cnt[tname]))
        elif tick:
            tname = own
            amt = 1
        for dd in force:
            if dd is None:
                continue
            s, c = dd
            assert c <= self.cnt[s], (f"future wait: {eng} {s} {c} > {self.cnt[s]}")
            if c > self.hwm[eng][s]:
                self.hwm[eng][s] = c
                waits.append((s, c))
        alldeps = list(deps) + self.pending[eng]
        self.pending[eng] = []
        for dd in alldeps:
            if dd is None:
                continue
            s, c = dd
            assert c <= self.cnt[s], (f"future wait: {eng} {s} {c} > {self.cnt[s]}")
            if s == own or (s in qsems and not dma):
                continue
            if c > self.hwm[eng][s]:
                self.hwm[eng][s] = c
                waits.append((s, c))
        tok = None
        if tname is not None:
            self.cnt[tname] += amt
            tok = (tname, self.cnt[tname])
        sems = self.sems

        def run(e, waits=tuple(waits), fn=fn, tname=tname, amt=amt):
            for s, c in waits:
                e.wait_ge(sems[s], c)
            ins = fn(e)
            if tname is not None:
                ins.then_inc(sems[tname], amt)

        self.ops[eng].append(run)
        self.labels[eng].append((self.phase, getattr(self, "phase_shape", None)))
        return tok

    def now(self, sem):
        return (sem, self.cnt[sem])


class Bank:
    def __init__(self, tensor):
        self.t = tensor
        self.w = None
        self.r = []


def host_consts():
    c = {}
    c["identb"] = np.eye(128, dtype=BF)
    c["identf"] = np.eye(128, dtype=np.float32)
    rot = np.zeros((128, 128), np.float32)
    for o in (0, 64):
        for i in range(32):
            rot[o + 32 + i, o + i] = -1.0
            rot[o + i, o + 32 + i] = 1.0
    c["rotp"] = rot.astype(BF)
    sel = np.zeros((8, 128, 16), np.float32)
    selt = np.zeros((8, 16, 128), np.float32)
    for t in range(8):
        for p in range(128):
            h = 2 * t + p // 64
            sel[t, p, h] = 1.0
            selt[t, h, p] = 1.0
    c["sel"] = sel.astype(BF)
    c["selt"] = selt.astype(BF)
    c["onescol"] = np.ones((128, 1), BF)
    c["onesrow"] = np.ones((1, 128), BF)
    i = np.arange(128)[:, None]
    w = np.arange(384)[None, :]
    c["mask_mid"] = np.where((w >= i) & (w <= i + 256), 0.0, NEG).astype(BF)
    w0 = np.arange(128)[None, :]
    c["mask_t0"] = np.where(w0 <= i, 0.0, NEG).astype(BF)
    w1 = np.arange(256)[None, :]
    c["mask_t1"] = np.where(w1 <= i + 128, 0.0, NEG).astype(BF)
    half = np.arange(0, DH, 2, dtype=np.float64)
    inv_freq = 1.0 / 10000.0 ** (half / DH)
    t = np.arange(S, dtype=np.float64)
    freqs = t[:, None] * inv_freq[None, :]
    freqs = np.concatenate([freqs, freqs], -1)
    base_scale = (half + 0.4 * DH) / (1.4 * DH)
    power = (t - S // 2) / (256 // 2)
    scale = base_scale[None, :] ** power[:, None]
    scale = np.concatenate([scale, scale], -1)
    cosf, sinf = np.cos(freqs), np.sin(freqs)
    for name, tab in [("cq", cosf * scale), ("sq", sinf * scale),
                      ("ck", cosf / scale), ("sk", sinf / scale)]:
        fm = tab.T.astype(BF)
        c[name] = np.concatenate([fm, fm], 0)
    return c


def prep_weights(inputs):
    d = {}
    W1 = np.asarray(inputs["L0_ffn_W1"])
    a = np.zeros((DEPTH6, D, PAD_INNER), np.float32)
    g = np.zeros((DEPTH6, D, PAD_INNER), np.float32)
    a[:, :, :DIM_INNER] = W1[:, :, :DIM_INNER]
    g[:, :, :DIM_INNER] = W1[:, :, DIM_INNER:]
    w1ag = np.empty((DEPTH6, D, NJ * 2, 128), np.float32)
    w1ag[:, :, 0::2, :] = a.reshape(DEPTH6, D, NJ, 128)
    w1ag[:, :, 1::2, :] = g.reshape(DEPTH6, D, NJ, 128)
    w1v = w1ag.reshape(DEPTH6, NKT, 128, NJ, 256)
    d["w1ag"] = np.ascontiguousarray(w1v.transpose(0, 3, 2, 1, 4).reshape(
        DEPTH6, NJ, 128, NKT * 256))
    W2 = np.asarray(inputs["L0_ffn_W2"])
    w2p = np.zeros((DEPTH6, PAD_INNER, D), np.float32)
    w2p[:, :DIM_INNER] = W2
    w2v = w2p.reshape(DEPTH6, NJ, 128, 8, 128)
    d["w2p"] = np.ascontiguousarray(w2v.transpose(0, 3, 2, 1, 4).reshape(
        DEPTH6, 8, 128, NJ * 128))
    b1 = np.asarray(inputs["L0_ffn_b1"])
    b1a = np.zeros((DEPTH6, PAD_INNER), np.float32)
    b1g = np.zeros((DEPTH6, PAD_INNER), np.float32)
    b1a[:, :DIM_INNER] = b1[:, :DIM_INNER]
    b1g[:, :DIM_INNER] = b1[:, DIM_INNER:]
    d["b1a_cols"] = np.ascontiguousarray(
        b1a.reshape(DEPTH6, NJ, 128).transpose(0, 2, 1))
    d["b1g_cols"] = np.ascontiguousarray(
        b1g.reshape(DEPTH6, NJ, 128).transpose(0, 2, 1))
    for name, key in [("lng", "L0_ffn_ln_g"), ("lnb", "L0_ffn_ln_b"),
                      ("b2", "L0_ffn_b2")]:
        v = np.asarray(inputs[key]).reshape(DEPTH6, 8, 128)
        d[name + "_cols"] = np.ascontiguousarray(v.transpose(0, 2, 1))
    qs = np.asarray(inputs["L0_ffn_q_scale"])
    ks = np.asarray(inputs["L0_ffn_k_scale"])
    d["qs_cols"] = np.ascontiguousarray(np.concatenate([qs, qs], 1).T)
    d["ks_cols"] = np.ascontiguousarray(np.concatenate([ks, ks], 1).T)
    Whg = np.asarray(inputs["L0_attn_W_hg"])
    whgi = np.empty((D, NET, 256), np.float32)
    whgi[:, :, 0:128] = Whg[:, 0:E_GRU].reshape(D, NET, 128)
    whgi[:, :, 128:256] = Whg[:, E_GRU:].reshape(D, NET, 128)
    whv = whgi.reshape(NKT, 128, NET, 256)
    d["whgi"] = np.ascontiguousarray(whv.transpose(2, 1, 0, 3).reshape(
        NET, 128, NKT * 256))
    wo = np.asarray(inputs["L0_attn_W_out"])
    wov = wo.reshape(NET, 128, 8, 128)
    d["wout"] = np.ascontiguousarray(wov.transpose(2, 1, 0, 3).reshape(
        8, 128, NET * 128))
    wq = np.asarray(inputs["L0_ffn_Wqkv"])
    wv = wq[:, :, 2048:3072].reshape(DEPTH6, NKT, 128, 2, 512)
    d["wvh"] = np.ascontiguousarray(wv.transpose(0, 3, 2, 1, 4).reshape(
        DEPTH6, 2, 128, NKT * 512))
    wo2 = np.asarray(inputs["L0_ffn_Wo"]).reshape(DEPTH6, NKT, 128, 2, 512)
    d["wob"] = np.ascontiguousarray(wo2.transpose(0, 3, 2, 1, 4).reshape(
        DEPTH6, 2, 128, NKT * 512))
    return d


def build(stop_after=None, taps=()):
    nc = bass.Bass("TRN2", target_bir_lowering=False)
    taps = set(taps)

    x_in = nc.dram_tensor("x", [S, D], F32, kind="ExternalInput").ap()
    wqkv = nc.dram_tensor("wqkv", [DEPTH6, D, 3072], F32, kind="ExternalInput").ap()
    w1ag = nc.dram_tensor("w1ag", [DEPTH6, NJ, 128, NKT * 256], F32, kind="ExternalInput").ap()
    w2p = nc.dram_tensor("w2p", [DEPTH6, 8, 128, NJ * 128], F32, kind="ExternalInput").ap()
    whgi = nc.dram_tensor("whgi", [NET, 128, NKT * 256], F32, kind="ExternalInput").ap()
    wout = nc.dram_tensor("wout", [8, 128, NET * 128], F32, kind="ExternalInput").ap()
    b1a_c = nc.dram_tensor("b1a_cols", [DEPTH6, 128, NJ], F32, kind="ExternalInput").ap()
    b1g_c = nc.dram_tensor("b1g_cols", [DEPTH6, 128, NJ], F32, kind="ExternalInput").ap()
    lng_c = nc.dram_tensor("lng_cols", [DEPTH6, 128, 8], F32, kind="ExternalInput").ap()
    lnb_c = nc.dram_tensor("lnb_cols", [DEPTH6, 128, 8], F32, kind="ExternalInput").ap()
    b2_c = nc.dram_tensor("b2_cols", [DEPTH6, 128, 8], F32, kind="ExternalInput").ap()
    qs_c = nc.dram_tensor("qs_cols", [128, DEPTH6], F32, kind="ExternalInput").ap()
    ks_c = nc.dram_tensor("ks_cols", [128, DEPTH6], F32, kind="ExternalInput").ap()
    wvh_d = nc.dram_tensor("wvh_d", [DEPTH6, 2, 128, NKT * 512], F32, kind="ExternalInput").ap()
    wob_d = nc.dram_tensor("wob_d", [DEPTH6, 2, 128, NKT * 512], F32, kind="ExternalInput").ap()
    cst = {}
    for name, shape, dt in [
        ("identb", [128, 128], BF16), ("identf", [128, 128], F32),
        ("rotp", [128, 128], BF16), ("sel", [8, 128, 16], BF16),
        ("selt", [8, 16, 128], BF16), ("onescol", [128, 1], BF16),
        ("onesrow", [1, 128], BF16), ("mask_mid", [128, 384], BF16),
        ("mask_t0", [128, 128], BF16), ("mask_t1", [128, 256], BF16),
        ("cq", [128, S], BF16), ("sq", [128, S], BF16),
        ("ck", [128, S], BF16), ("sk", [128, S], BF16),
    ]:
        cst[name] = nc.dram_tensor(name, shape, dt, kind="ExternalInput").ap()
    out_d = nc.dram_tensor("out", [S, D], F32, kind="ExternalOutput").ap()
    out_t = out_d.rearrange("(t p) d -> t p d", p=128)
    xin_t = x_in.rearrange("(t p) d -> t p d", p=128)
    wqkv_v = wqkv.rearrange("l (kt p) e -> l p kt e", p=128)

    top = contextlib.ExitStack()
    E = top.enter_context
    sems = {s: E(nc.semaphore(s)) for s in ALL_SEMS}
    pl = PL(sems)

    X = [E(nc.sbuf_tensor(f"X{d}", [128, S + 4], F32)) for d in range(NKT)]
    IDB = E(nc.sbuf_tensor("s_identb", [128, 128], BF16))
    IDF = E(nc.sbuf_tensor("s_identf", [128, 128], F32))
    ROTP = E(nc.sbuf_tensor("s_rotp", [128, 128], BF16))
    SEL = [E(nc.sbuf_tensor(f"s_sel{t}", [128, 16], BF16)) for t in range(8)]
    SELT = [E(nc.sbuf_tensor(f"s_selt{t}", [16, 128], BF16)) for t in range(8)]
    OC = E(nc.sbuf_tensor("s_onescol", [128, 1], BF16))
    OR = E(nc.sbuf_tensor("s_onesrow", [1, 128], BF16))
    NMU = E(nc.sbuf_tensor("negmu", [128, NTT], F32))
    RSD = E(nc.sbuf_tensor("rstd", [128, NTT], F32))
    QS8 = E(nc.sbuf_tensor("qs8", [128, DEPTH6], F32))
    KS1 = E(nc.sbuf_tensor("ks1", [128, DEPTH6], F32))
    HSTA = E(nc.sbuf_tensor("hst_a", [128, 3 * NKT], F32))
    HSTF = E(nc.sbuf_tensor("hst_f", [128, NKT], F32))
    ZCOL = E(nc.sbuf_tensor("zcol", [128, 1], F32))
    EPSC = E(nc.sbuf_tensor("epsc", [128, 1], F32))
    lyr_scope = contextlib.ExitStack()
    EL_ = lyr_scope.enter_context
    MMK = EL_(nc.sbuf_tensor("maskm", [128, 384], BF16))
    TAB = {k: EL_(nc.sbuf_tensor("tab_" + k, [128, S], BF16))
           for k in ("cq", "sq", "ck", "sk")}

    PS = [Bank(E(nc.psum_tensor(f"ps{i}", [128, 512], F32))) for i in range(6)]
    PTB2 = [Bank(E(nc.psum_tensor(f"ptb{i}", [128, 768], BF16))) for i in range(2)]
    PTB = PTB2[0]
    PV0 = Bank(PS[0].t)
    PV1 = Bank(PS[1].t)

    def dma(dst_ap, src_ap, deps=(), q="sp"):
        return pl.op(q, lambda e, d=dst_ap, s=src_ap: e.dma_start(out=d, in_=s),
                     deps=deps, dma=True)

    def mm(bank, out_ap, lh, rh, start, stop, deps=(), tick=False):
        deps = list(deps)
        if start:
            deps += bank.r
            bank.r = []
        pl.phase_shape = f"{lh.shape[0]}*{lh.shape[-1]}*{rh.shape[-1]}"
        tok = pl.op("pe", lambda e, o=out_ap, a=lh, b=rh, st=start, sp=stop:
                    e.matmul(o, a, b, start=st, stop=sp), deps=deps,
                    tick=(tick or stop))
        pl.phase_shape = None
        if stop:
            bank.w = tok
        return tok

    def transp(bank, out_ap, in_ap, ident, deps=(), first=True, last=True):
        deps = list(deps)
        if first:
            deps += bank.r
            bank.r = []
        pl.phase_shape = "T"
        tok = pl.op("pe", lambda e, o=out_ap, i=in_ap, d=ident:
                    e.transpose(o, i, d), deps=deps, tick=True)
        pl.phase_shape = None
        if last:
            bank.w = tok
        return tok

    def rd(eng, bank, fn, deps=(), tick=True, force=()):
        tok = pl.op(eng, fn, deps=list(deps) + [bank.w], tick=tick, force=force)
        bank.r.append(tok)
        return tok

    for name, buf in [("mask_mid", MMK),
                      ("identb", IDB), ("identf", IDF), ("rotp", ROTP),
                      ("onescol", OC), ("onesrow", OR)]:
        dma(buf[:], cst[name])
    for k in TAB:
        dma(TAB[k][:], cst[k])
    for t in range(8):
        dma(SEL[t][:], cst["sel"][t])
        dma(SELT[t][:], cst["selt"][t])
    dma(QS8[:], qs_c)
    dma(KS1[:], ks_c)
    consts_toks = [pl.now(s) for s in QSEMS["sp"]]
    pl.op("dve", lambda e: e.tensor_scalar(QS8[:], QS8[:], 8.0, None, ALU.mult),
          deps=consts_toks, tick=True)
    pl.op("dve", lambda e: e.memset(ZCOL[:], 0.0), tick=True)
    pl.op("dve", lambda e: e.memset(EPSC[:], 1e-5), tick=True)

    finish_deps = []

    def phase0():
        with contextlib.ExitStack() as ph:
            E0 = ph.enter_context
            pl.phase = "p0"
            pl.fence()
            XT = [E0(nc.sbuf_tensor(f"xt{t}", [128, D], F32)) for t in range(NTT)]
            SQ0 = E0(nc.sbuf_tensor("sq0", [128, D], F32))
            CLS = E0(nc.sbuf_tensor("p0cols", [128, 8], F32))
            XNT = E0(nc.sbuf_tensor("xnt", [128, D], F32))
            tok_x = [dma(XT[t][:], xin_t[t]) for t in range(NTT)]
            stat_rs = []
            tk_ln = None
            tk_red = None
            for t in range(NTT):
                tk_r0 = pl.op("dve", lambda e, t=t: e.reduce_sum(
                    CLS[:, 0:1], XT[t][:], AX.X), deps=[tok_x[t]], tick=True)
                tk_sq = pl.op("act", lambda e, t=t: e.activation(
                    SQ0[:], XT[t][:], AF.Square), deps=[tok_x[t], tk_red], tick=True)
                tk_red = pl.op("dve", lambda e: e.reduce_sum(CLS[:, 1:2], SQ0[:], AX.X),
                               deps=[tk_sq], tick=True)
                pl.op("dve", lambda e, t=t: e.tensor_scalar(
                    NMU[:, t:t + 1], CLS[:, 0:1], -1.0 / D, None, ALU.mult),
                    force=[tk_r0])
                tk_m2 = pl.op("dve", lambda e: e.tensor_tensor(
                    CLS[:, 2:3], CLS[:, 0:1], CLS[:, 0:1], ALU.mult),
                    force=[tk_r0], tick=True)
                tk_m3 = pl.op("dve", lambda e: e.tensor_scalar(
                    CLS[:, 3:4], CLS[:, 2:3], 1.0 / (float(D) * D), None, ALU.mult),
                    force=[tk_m2], tick=True)
                tk_var = pl.op("dve", lambda e: e.scalar_tensor_tensor(
                    CLS[:, 4:5], CLS[:, 1:2], 1.0 / D, CLS[:, 3:4],
                    ALU.mult, ALU.subtract), deps=[tk_ln],
                    force=[tk_m3, tk_red], tick=True)
                tk_ln = pl.op("act", lambda e: e.activation(
                    CLS[:, 5:6], CLS[:, 4:5], AF.Ln, bias=EPSC[:], scale=1.0),
                    deps=[tk_var], tick=True)
                tk_rs = pl.op("act", lambda e, t=t: e.activation(
                    RSD[:, t:t + 1], CLS[:, 5:6], AF.Exp, scale=-0.5),
                    deps=[tk_ln], force=[tk_ln], tick=True)
                stat_rs.append(tk_rs)
            for d in range(NKT):
                pl.op("dve", lambda e, d=d: e.memset(X[d][:, 0:1], 0.0))
            xnt_used = None
            for t in range(NTT):
                tk_n = pl.op("dve", lambda e, t=t: e.tensor_scalar(
                    XNT[:], XT[t][:], NMU[:, t:t + 1], RSD[:, t:t + 1],
                    ALU.add, ALU.mult), deps=[stat_rs[t], xnt_used], tick=True)
                for d in range(NKT):
                    bk = PS[d % 2]
                    xnt_used = transp(bk, bk.t[:, 0:128],
                                      XNT[:, d * 128:(d + 1) * 128], IDF[:],
                                      deps=[tk_n])
                    rd("act", bk, lambda e, b=bk, t=t, d=d: e.copy(
                        X[d][:, 1 + t * 128: 1 + (t + 1) * 128], b.t[:, 0:128]))

    wstate = dict(wqk=None, wvh=[None, None], wob=[None, None],
                  w1=[None, None, None, None], w2=[None, None],
                  whg=[None, None], wos=[None, None])

    def attn_phase(l, substop=None):
        with contextlib.ExitStack() as ph:
            EA = ph.enter_context
            pl.fence()
            XB = [EA(nc.sbuf_tensor(f"axb{d}_l{l}", [128, 516], BF16)) for d in range(NKT)]
            WQK = EA(nc.sbuf_tensor(f"wqk_l{l}", [128, 8 * 2048], BF16))
            WVH = [EA(nc.sbuf_tensor(f"wvh{i}_l{l}", [128, 8 * 512], BF16))
                   for i in range(2)]
            WOB = [EA(nc.sbuf_tensor(f"wob{i}_l{l}", [128, 8 * 512], BF16))
                   for i in range(2)]
            QR = [EA(nc.sbuf_tensor(f"qr{m}_l{l}", [128, 512], BF16)) for m in range(8)]
            KR = [EA(nc.sbuf_tensor(f"kr{m}_l{l}", [128, 768], BF16)) for m in range(8)]
            VR = [EA(nc.sbuf_tensor(f"vrr{s}_l{l}", [128, 1024], BF16)) for s in range(6)]
            AC = [EA(nc.sbuf_tensor(f"acc{d}_l{l}", [128, 512], BF16)) for d in range(NKT)]
            SSF = EA(nc.sbuf_tensor(f"ssf_l{l}", [16, 512], F32))
            NIV = EA(nc.sbuf_tensor(f"niv_l{l}", [16, 512], BF16))
            PN = [EA(nc.sbuf_tensor(f"pn{i}_l{l}", [128, 768], BF16))
                  for i in range(2)]
            PTS = [EA(nc.sbuf_tensor(f"pts{i}_l{l}", [128, 768], BF16))
                   for i in range(2)]
            NML = EA(nc.sbuf_tensor(f"nml_l{l}", [128, 16], F32))
            RCL = EA(nc.sbuf_tensor(f"rcl_l{l}", [128, 4], F32))

            wqk_src = wqkv_v[l][:, :, 0:2048]
            wqk_dst = WQK[:].rearrange("p (kt e) -> p kt e", e=2048)
            wstate["wqk"] = dma(wqk_dst, wqk_src, deps=[wstate["wqk"]], q="pool")
            tok_wqk = wstate["wqk"]

            dve_now = pl.now("dve")
            stash = []
            for d in range(NKT):
                for i in range(3):
                    stash.append(pl.op("act", lambda e, d=d, i=i: e.copy(
                        HSTA[:, 3 * d + i:3 * d + i + 1],
                        X[d][:, (i + 1) * 512:(i + 1) * 512 + 1]),
                        deps=[dve_now], tick=True))

            for c in range(4):
                if c > 0:
                    pl.fence()
                base = c * 512
                xb_toks = []
                for d in range(NKT):
                    if c == 0:
                        pl.op("dve", lambda e, d=d: e.tensor_copy(
                            XB[d][:, 0:1], X[d][:, 0:1]))
                    else:
                        pl.op("dve", lambda e, d=d, c=c: e.tensor_copy(
                            XB[d][:, 0:1], HSTA[:, 3 * d + c - 1:3 * d + c]),
                            deps=[stash[3 * d + c - 1]])
                    xb_toks.append(pl.op("dve", lambda e, d=d, base=base: e.tensor_copy(
                        XB[d][:, 1:513], X[d][:, base + 1:base + 513]), tick=True))

                def xs_ap(kt, lo, hi):
                    off = 0 if kt >= 4 else 1
                    return XB[kt][:, lo + off:hi + off]

                if substop == "xb":
                    return

                if c > 0:
                    for m in range(8):
                        pl.op("dve", lambda e, m=m: e.tensor_copy(
                            KR[m][:, 0:256], KR[m][:, 512:768]), tick=(m == 7))
                    for s in range(2):
                        pl.op("dve", lambda e, s=s: e.tensor_copy(
                            VR[s][:], VR[4 + s][:]), tick=(s == 1))
                roll_tok = pl.now("dve")

                pl.phase = f"attn.prep"
                qsq_used = [None, None]
                perm_used = {}
                sq_done = {}
                PARTS = [(0, QS8, "cq", "sq"), (1024, KS1, "ck", "sk")]
                cp_toks = {}

                def raw_ap(part, m):
                    return QR[m][:, 0:512] if part == 0 else KR[m][:, 256:768]

                for part, (eoff, scl, tabc, tabs_) in enumerate(PARTS):
                    for m in range(8):
                        bk = PS[m % 4]
                        for kt in range(NKT):
                            mm(bk, bk.t[:], WQK[:, kt * 2048 + eoff + m * 128:
                                                 kt * 2048 + eoff + (m + 1) * 128],
                               xs_ap(kt, 0, 512), kt == 0, kt == 7,
                               deps=[tok_wqk, xb_toks[kt]])
                        cp_toks[(part, m)] = rd(
                            "dve", bk, lambda e, b=bk, a=raw_ap(part, m):
                            e.tensor_copy(a, b.t[:]),
                            deps=([roll_tok] if part == 1 else []))

                def sumsq_group(part):
                    for m in range(8):
                        tk_sq = pl.op("act", lambda e, a=raw_ap(part, m), m=m:
                                      e.activation(PN[m % 2][:, 0:512], a, AF.Square),
                                      deps=[cp_toks[(part, m)], qsq_used[m % 2]],
                                      tick=True)
                        sq_done[(part, m)] = tk_sq
                        qsq_used[m % 2] = mm(
                            PS[4], PS[4].t[0:16, 0:512], SEL[m][:],
                            PN[m % 2][:, 0:512], m == 0, m == 7, deps=[tk_sq],
                            tick=True)
                    tk_ssf = rd("dve", PS[4], lambda e: e.tensor_copy(
                        SSF[:], PS[4].t[0:16, 0:512]))
                    tk_mx = pl.op("dve", lambda e: e.tensor_scalar(
                        SSF[:], SSF[:], 1e-24, None, ALU.max), force=[tk_ssf],
                        tick=True)
                    tk_ln = pl.op("act", lambda e: e.activation(
                        SSF[:], SSF[:], AF.Ln), deps=[tk_mx], tick=True)
                    return pl.op("act", lambda e: e.activation(
                        NIV[:], SSF[:], AF.Exp, scale=-0.5), deps=[tk_ln],
                        force=[tk_ln], tick=True)

                def v_group_tt(eh, tt, tok_wv):
                    bk = PS[tt % 2]
                    for kt in range(NKT):
                        off = 0 if kt >= 4 else 1
                        mm(bk, bk.t[:], XB[kt][:, tt * 128 + off:tt * 128 + off + 128],
                           WVH[eh][:, kt * 512:(kt + 1) * 512], kt == 0, kt == 7,
                           deps=[tok_wv, xb_toks[kt], roll_tok])
                    rd("dve", bk, lambda e, b=bk, tt=tt, eh=eh: e.tensor_copy(
                        VR[2 + tt][:, eh * 512:(eh + 1) * 512], b.t[:]))

                def scale_rot(part, tk_niv, eh, tok_wv):
                    eoff, scl, tabc, tabs_ = PARTS[part]
                    for m in range(8):
                        a = raw_ap(part, m)
                        bb = PS[4 + (m % 2)] if part == 0 else PS[5 - (m % 2)]
                        mm(bb, bb.t[:], SELT[m][:], NIV[:], True, True,
                           deps=[tk_niv])
                        tk_scale = rd("dve", bb, lambda e, b=bb, a=a, scl=scl, l=l:
                                      e.scalar_tensor_tensor(
                            a, b.t[:], scl[:, l:l + 1], a, ALU.mult, ALU.mult),
                            deps=[sq_done[(part, m)]])
                        rb = PS[2 + (m % 2)]
                        tk_perm = mm(rb, rb.t[:], ROTP[:], a, True, True,
                                     deps=[tk_scale])
                        rd("dve", rb, lambda e, b=rb, tn=tabs_, base=base:
                           e.tensor_tensor(PTS[0][:, 0:512], b.t[:],
                                           TAB[tn][:, base:base + 512], ALU.mult))
                        pl.op("dve", lambda e, a=a, tn=tabc, base=base:
                              e.tensor_tensor(a, a,
                                              TAB[tn][:, base:base + 512], ALU.mult),
                              deps=[tk_perm])
                        pl.op("dve", lambda e, a=a: e.tensor_tensor(
                            a, a, PTS[0][:, 0:512], ALU.add), tick=True)
                        if m in (3, 5, 7):
                            v_group_tt(eh, (m - 1) // 2, tok_wv)

                tok_wv = [None, None]
                for eh in range(2):
                    tok_wv[eh] = dma(WVH[eh][:], wvh_d[l][eh],
                                     deps=[wstate["wvh"][eh]], q="pool")
                tk_nivq = sumsq_group(0)
                v_group_tt(0, 0, tok_wv[0])
                scale_rot(0, tk_nivq, 0, tok_wv[0])
                wstate["wvh"][0] = pl.now("pe")
                tk_nivk = sumsq_group(1)
                v_group_tt(1, 0, tok_wv[1])
                scale_rot(1, tk_nivk, 1, tok_wv[1])
                wstate["wvh"][1] = pl.now("pe")
                qk_done = pl.now("dve")
                v_done = pl.now("dve")
                prep_pe = pl.now("pe")
                if substop == "v":
                    return

                pl.phase = "attn.units"
                steps = [(hp, qi, hi) for hp in range(8) for qi in range(4)
                         for hi in range(2)]

                def unit_geom(qi):
                    if c == 0 and qi < 2:
                        w = 128 * (qi + 1)
                        rc0 = 256
                        mask = MMK[:, 384 - w:384]
                    else:
                        w = 384
                        rc0 = qi * 128
                        mask = MMK[:, 0:384]
                    njb = w // 128
                    return w, rc0, mask, njb, qi + (3 - njb)

                stash_u = {}

                def pn_ap(h, w):
                    r = h % 4
                    return PN[r // 2][:, (r % 2) * 384:(r % 2) * 384 + w]

                def pts_ap(h, lo, hi_):
                    r = h % 4
                    return PTS[r // 2][:, (r % 2) * 384 + lo:(r % 2) * 384 + hi_]

                def emit_sim(h):
                    hp, qi, hi = steps[h]
                    hb = 64 * hi
                    w, rc0, mask, njb, slot0 = unit_geom(qi)
                    bk = PS[2 + h % 4]
                    mm(bk, bk.t[:, 0:w], IDB[:], mask, True, False)
                    mm(bk, bk.t[:, 0:w],
                       QR[hp][hb:hb + 64, qi * 128:(qi + 1) * 128],
                       KR[hp][hb:hb + 64, rc0:rc0 + w], False, True,
                       deps=[qk_done])
                    nm = rd("dve", bk, lambda e, b=bk, h=h, w=w:
                            e.reduce_max(NML[:, (h % 4) * 2:(h % 4) * 2 + 1],
                                         b.t[:, 0:w], AX.X, negate=True))
                    tk_exp = rd("act", bk, lambda e, b=bk, h=h, w=w:
                                e.activation(
                        pn_ap(h, w), b.t[:, 0:w], AF.Exp,
                        bias=NML[:, (h % 4) * 2:(h % 4) * 2 + 1],
                        accum_out=NML[:, (h % 4) * 2 + 1:(h % 4) * 2 + 2]),
                        deps=[nm, stash_u.get(h - 4, {}).get("tr_last"),
                              (prep_pe if h < 4 else None)])
                    tk_rcl = pl.op("dve", lambda e, h=h: e.reciprocal(
                        RCL[:, h % 4:h % 4 + 1],
                        NML[:, (h % 4) * 2 + 1:(h % 4) * 2 + 2]),
                        deps=[tk_exp], tick=True)
                    tk_sc = pl.op("act", lambda e, h=h, w=w: e.activation(
                        pn_ap(h, w), pn_ap(h, w), AF.Copy,
                        scale=RCL[:, h % 4:h % 4 + 1]),
                        deps=[tk_exp, tk_rcl], tick=True)
                    stash_u[h] = dict(sc=tk_sc, njb=njb, w=w, slot0=slot0,
                                      hp=hp, qi=qi, hi=hi)

                def emit_trans(h):
                    st_ = stash_u[h]
                    njb = st_["njb"]
                    ptb = PTB2[h % 2]
                    pts_war = stash_u.get(h - 4, {}).get("pv_last")
                    trs = []
                    for jb in range(njb):
                        trs.append(transp(
                            ptb, ptb.t[:, jb * 128:(jb + 1) * 128],
                            pn_ap(h, st_["w"])[:, jb * 128:(jb + 1) * 128],
                            IDB[:], deps=[st_["sc"]], first=(jb == 0),
                            last=(jb == njb - 1)))
                    st_["tr_last"] = trs[-1]
                    st_["ptse"] = rd("dve", ptb, lambda e, h=h, njb=njb:
                                     e.tensor_copy(pts_ap(h, 0, njb * 128),
                                                   ptb.t[:, 0:njb * 128]),
                                     deps=[pts_war])

                def emit_pv(h):
                    st_ = stash_u[h]
                    hp, qi, hi = st_["hp"], st_["qi"], st_["hi"]
                    njb, slot0 = st_["njb"], st_["slot0"]
                    hb = 64 * hi
                    pvb = [PV0, PV1][h % 2]
                    for jb in range(njb):
                        st_["pv_last"] = mm(
                            pvb, PS[h % 2].t[hb:hb + 64, 0:128],
                            VR[slot0 + jb][:, (2 * hp + hi) * 64:(2 * hp + hi + 1) * 64],
                            pts_ap(h, jb * 128, (jb + 1) * 128),
                            jb == 0, jb == njb - 1,
                            deps=[st_["ptse"], v_done])
                    tk_a = pl.op("act", lambda e, hp=hp, qi=qi, hb=hb, h=h: e.copy(
                        AC[hp][hb:hb + 64, qi * 128:(qi + 1) * 128],
                        PS[h % 2].t[hb:hb + 64, 0:128]),
                        deps=[pvb.w], tick=True)
                    pvb.r.append(tk_a)
                    stash_u.pop(h - 4, None)

                for h in range(len(steps)):
                    emit_sim(h)
                    if h > 1:
                        emit_trans(h - 2)
                    if h > 2:
                        emit_pv(h - 3)
                for h in (len(steps) - 2, len(steps) - 1):
                    emit_trans(h)
                for h in (len(steps) - 3, len(steps) - 2, len(steps) - 1):
                    emit_pv(h)
                a_done = pl.now("act")
                if substop == "units":
                    return

                pl.phase = "attn.wo"
                res_toks = []
                tok_wob = [None, None]
                for hf in range(2):
                    tok_wob[hf] = dma(WOB[hf][:], wob_d[l][hf],
                                      deps=[wstate["wob"][hf]], q="pool")
                for m in range(8):
                    bk = PS[m % 4]
                    for kt in range(NKT):
                        mm(bk, bk.t[:], WOB[m // 4][:, kt * 512 + (m % 4) * 128:
                                                    kt * 512 + (m % 4) * 128 + 128],
                           AC[kt][:, 0:512], kt == 0, kt == 7,
                           deps=[tok_wob[m // 4], a_done])
                    if m == 3:
                        wstate["wob"][0] = pl.now("pe")
                    if m == 7:
                        wstate["wob"][1] = pl.now("pe")
                    res_toks.append(rd("dve", bk, lambda e, b=bk, m=m, base=base:
                                       e.scalar_tensor_tensor(
                        X[m][:, base + 1:base + 513], b.t[:], 1.0,
                        X[m][:, base + 1:base + 513], ALU.mult, ALU.add),
                        deps=stash))
            wstate["wqk"] = pl.now("pe")

    def ffn_phase(l):
        with contextlib.ExitStack() as ph:
            EF = ph.enter_context
            pl.fence()
            XB2 = [EF(nc.sbuf_tensor(f"fxb{d}_l{l}", [128, 1028], BF16)) for d in range(NKT)]
            GAT = EF(nc.sbuf_tensor(f"gated_l{l}", [128, NJ * 1024], BF16))
            SQF = [EF(nc.sbuf_tensor(f"sqf{i}_l{l}", [128, 512], BF16)) for i in range(2)]
            XN2 = [EF(nc.sbuf_tensor(f"xn2_{d}_l{l}", [128, 1024], BF16)) for d in range(NKT)]
            W1S = [EF(nc.sbuf_tensor(f"w1s{i}_l{l}", [128, 8 * 256], BF16)) for i in range(4)]
            W2S = [EF(nc.sbuf_tensor(f"w2s{i}_l{l}", [128, NJ * 128], BF16)) for i in range(2)]
            ROWS = EF(nc.sbuf_tensor(f"rows_l{l}", [1, 5 * 512], F32))
            ROWB = EF(nc.sbuf_tensor(f"rowb_l{l}", [1, 2 * 512], BF16))
            TN = [EF(nc.sbuf_tensor(f"tn{i}_l{l}", [128, 512], F32)) for i in range(2)]
            GG = EF(nc.sbuf_tensor(f"gg_l{l}", [128, 512], BF16))
            B1A = EF(nc.sbuf_tensor(f"b1a_l{l}", [128, NJ], F32))
            B1G = EF(nc.sbuf_tensor(f"b1g_l{l}", [128, NJ], F32))
            LNG = EF(nc.sbuf_tensor(f"lng_l{l}", [128, 8], F32))
            LNB = EF(nc.sbuf_tensor(f"lnb_l{l}", [128, 8], F32))
            B2C = EF(nc.sbuf_tensor(f"b2c_l{l}", [128, 8], F32))
            vec_toks = [dma(B1A[:], b1a_c[l]), dma(B1G[:], b1g_c[l]),
                        dma(LNG[:], lng_c[l]), dma(LNB[:], lnb_c[l]),
                        dma(B2C[:], b2_c[l])]
            pl.pending["dve"].extend(vec_toks)
            pl.pending["act"].extend(vec_toks)

            dve_now = pl.now("dve")
            stash = [pl.op("act", lambda e, d=d: e.copy(
                HSTF[:, d:d + 1], X[d][:, 1024:1025]), deps=[dve_now], tick=True)
                for d in range(NKT)]
            for cc in range(2):
                pl.phase = "ffn.ln"
                if cc > 0:
                    pl.fence()
                fb = cc * 1024
                xb_toks = []
                for d in range(NKT):
                    if cc == 0:
                        pl.op("dve", lambda e, d=d: e.tensor_copy(
                            XB2[d][:, 0:1], X[d][:, 0:1]))
                    else:
                        pl.op("dve", lambda e, d=d: e.tensor_copy(
                            XB2[d][:, 0:1], HSTF[:, d:d + 1]), deps=[stash[d]])
                    xb_toks.append(pl.op("dve", lambda e, d=d, fb=fb: e.tensor_copy(
                        XB2[d][:, 1:1025], X[d][:, fb + 1:fb + 1025]), tick=True))

                def xs2(kt, lo, hi):
                    off = 0 if kt >= 4 else 1
                    return XB2[kt][:, lo + off:hi + off]

                xn_done_sc = [None, None]
                xn_toks = []
                for sc in range(2):
                    scb = sc * 512
                    sq_toks = [None, None]
                    bkx, bkq = PS[2], PS[3]
                    for kt in range(NKT):
                        mm(bkx, bkx.t[0:1, 0:512], OC[:], xs2(kt, scb, scb + 512),
                           kt == 0, kt == 7, deps=[xb_toks[kt]], tick=(kt == 7))
                    for kt in range(NKT):
                        tk_s = pl.op("act", lambda e, kt=kt, scb=scb: e.activation(
                            SQF[kt % 2][:], xs2(kt, scb, scb + 512), AF.Square),
                            deps=[xb_toks[kt], sq_toks[kt % 2]], tick=True)
                        sq_toks[kt % 2] = mm(bkq, bkq.t[0:1, 0:512], OC[:],
                                             SQF[kt % 2][:], kt == 0, kt == 7,
                                             deps=[tk_s], tick=True)
                    R = [ROWS[0:1, i * 512:(i + 1) * 512] for i in range(5)]
                    tk0 = rd("dve", bkx, lambda e, R=R, b=bkx: e.tensor_scalar(
                        R[0], b.t[0:1, 0:512], -1.0 / D, None, ALU.mult))
                    tk_m2 = pl.op("dve", lambda e, R=R: e.tensor_tensor(
                        R[1], R[0], R[0], ALU.mult), force=[tk0], tick=True)
                    tk_var = rd("dve", bkq, lambda e, R=R, b=bkq: e.scalar_tensor_tensor(
                        R[2], b.t[0:1, 0:512], 1.0 / D, R[1], ALU.mult,
                        ALU.subtract), force=[tk_m2])
                    tk_ln = pl.op("act", lambda e, R=R: e.activation(
                        R[3], R[2], AF.Ln, bias=EPSC[0:1, 0:1], scale=1.0),
                        deps=[tk_var], tick=True)
                    tk_rsd = pl.op("act", lambda e, R=R: e.activation(
                        R[4], R[3], AF.Exp, scale=-0.5), deps=[tk_ln],
                        force=[tk_ln], tick=True)
                    pl.op("dve", lambda e, R=R: e.tensor_copy(ROWB[0:1, 0:512], R[0]),
                          deps=[tk0], force=[tk0])
                    tk_rb = pl.op("dve", lambda e, R=R: e.tensor_copy(
                        ROWB[0:1, 512:1024], R[4]), deps=[tk_rsd], tick=True)
                    mm(bkx, bkx.t[:], OR[:], ROWB[0:1, 0:512], True, True,
                       deps=[tk_rb])
                    mm(bkq, bkq.t[:], OR[:], ROWB[0:1, 512:1024], True, True,
                       deps=[tk_rb])
                    for kt in range(NKT):
                        t1 = rd("dve", bkx, lambda e, kt=kt, scb=scb, b=bkx:
                                e.tensor_tensor(
                            TN[0][:], xs2(kt, scb, scb + 512), b.t[:], ALU.add),
                            tick=True)
                        rd("dve", bkq, lambda e, kt=kt, l=l, b=bkq:
                           e.scalar_tensor_tensor(
                            TN[1][:], b.t[:], LNG[:, kt:kt + 1],
                            TN[0][:], ALU.mult, ALU.mult))
                        xn_toks.append(pl.op("dve", lambda e, kt=kt, scb=scb, l=l:
                                             e.tensor_scalar(
                            XN2[kt][:, scb:scb + 512], TN[1][:],
                            LNB[:, kt:kt + 1], None, ALU.add),
                            tick=True))
                    xn_done_sc[sc] = pl.now("dve")

                pl.phase = "ffn.w1"
                gat_sc = [None, None]
                for j in range(NJ):
                    wstate["w1"][j % 4] = dma(
                        W1S[j % 4][:], w1ag[l][j],
                        deps=[wstate["w1"][j % 4]], q="pool")
                    tok_w1 = wstate["w1"][j % 4]
                    for sc in range(2):
                        scb = sc * 512
                        ba, bg = PS[2 * sc], PS[2 * sc + 1]
                        for kt in range(NKT):
                            mm(ba, ba.t[:], W1S[j % 4][:, kt * 256:kt * 256 + 128],
                               XN2[kt][:, scb:scb + 512], kt == 0, kt == 7,
                               deps=[tok_w1, xn_done_sc[sc]])
                        for kt in range(NKT):
                            mm(bg, bg.t[:], W1S[j % 4][:, kt * 256 + 128:kt * 256 + 256],
                               XN2[kt][:, scb:scb + 512], kt == 0, kt == 7,
                               deps=[tok_w1, xn_done_sc[sc]], tick=(kt == 7))
                        tk_gg = rd("act", bg, lambda e, b=bg, j=j, l=l: e.activation(
                            GG[:], b.t[:], AF.Gelu,
                            bias=B1G[:, j:j + 1]))
                        gat_sc[sc] = rd("dve", ba, lambda e, b=ba, j=j, sc=sc, l=l:
                           e.scalar_tensor_tensor(
                            GAT[:, j * 1024 + sc * 512:j * 1024 + sc * 512 + 512],
                            b.t[:], B1A[:, j:j + 1], GG[:],
                            ALU.add, ALU.mult), deps=[tk_gg])
                    wstate["w1"][j % 4] = pl.now("pe")
                if "ffndbg" in taps and l == 0 and cc == 0:
                    tpr = nc.dram_tensor("tap_rows", [1, 5 * 512], F32,
                                         kind="ExternalOutput").ap()
                    dma(tpr, ROWS[0:1, 0:5 * 512], deps=[pl.now("dve"), pl.now("act")])
                    tpx = nc.dram_tensor("tap_xn2", [NKT, 128, 1024], BF16,
                                         kind="ExternalOutput").ap()
                    tpg = nc.dram_tensor("tap_gat", [128, NJ * 1024], BF16,
                                         kind="ExternalOutput").ap()
                    deps = [pl.now(s) for s in ("pe", "dve", "act")]
                    for d in range(NKT):
                        dma(tpx[d], XN2[d][:], deps=deps)
                    dma(tpg, GAT[:], deps=deps)

                pl.phase = "ffn.w2"
                for m in range(8):
                    wstate["w2"][m % 2] = dma(
                        W2S[m % 2][:], w2p[l][m],
                        deps=[wstate["w2"][m % 2]], q="pool")
                    tok_w2 = wstate["w2"][m % 2]
                    for sc in range(2):
                        bk = PS[(2 * m + sc) % 4]
                        for j in range(NJ):
                            mm(bk, bk.t[:], W2S[m % 2][:, j * 128:(j + 1) * 128],
                               GAT[:, j * 1024 + sc * 512:j * 1024 + sc * 512 + 512],
                               j == 0, j == NJ - 1, deps=[tok_w2, gat_sc[sc]])
                        rd("dve", bk, lambda e, b=bk, m=m, fb=fb, sc=sc, l=l:
                           e.scalar_tensor_tensor(
                            X[m][:, fb + sc * 512 + 1:fb + sc * 512 + 513], b.t[:],
                            B2C[:, m:m + 1],
                            X[m][:, fb + sc * 512 + 1:fb + sc * 512 + 513],
                            ALU.add, ALU.add), deps=stash)
                    wstate["w2"][m % 2] = pl.now("pe")

    def mingru(silu_toks):
        with contextlib.ExitStack() as ph:
            EM = ph.enter_context
            pl.phase = "gru.xnb"
            pl.fence()
            H = EM(nc.sbuf_tensor("hbuf", [128, NET * S], BF16))
            with contextlib.ExitStack() as ph2:
                E2 = ph2.enter_context
                XNB = [E2(nc.sbuf_tensor(f"xnb{d}", [128, S], BF16)) for d in range(NKT)]
                XTM = [E2(nc.sbuf_tensor(f"mxt{i}", [128, D], F32)) for i in range(2)]
                XNTB = E2(nc.sbuf_tensor("xntb", [128, D], BF16))
                xtm_used = [None, None]
                xntb_used = None
                for t in range(NTT):
                    tk_l = dma(XTM[t % 2][:], xin_t[t], deps=[xtm_used[t % 2]])
                    tk_n = pl.op("dve", lambda e, t=t: e.tensor_scalar(
                        XNTB[:], XTM[t % 2][:], NMU[:, t:t + 1], RSD[:, t:t + 1],
                        ALU.add, ALU.mult), deps=[tk_l, xntb_used], tick=True)
                    xtm_used[t % 2] = tk_n
                    for d in range(NKT):
                        reg = d % 2
                        xntb_used = transp(PTB, PTB.t[:, reg * 128:(reg + 1) * 128],
                                           XNTB[:, d * 128:(d + 1) * 128], IDB[:],
                                           deps=[tk_n], first=True, last=True)
                        rd("act", PTB, lambda e, d=d, t=t, reg=reg: e.copy(
                            XNB[d][:, t * 128:(t + 1) * 128],
                            PTB.t[:, reg * 128:(reg + 1) * 128]))
                xnb_done = pl.now("act")

                WHGS = [E2(nc.sbuf_tensor(f"whgs{i}", [128, 8 * 256], BF16))
                        for i in range(2)]
                CB = E2(nc.sbuf_tensor("cbuf", [128, 512], F32))
                ZB = E2(nc.sbuf_tensor("zbuf", [128, 512], F32))
                RB = E2(nc.sbuf_tensor("rbuf", [128, 512], F32))
                VB = E2(nc.sbuf_tensor("vbuf", [128, 512], F32))
                SCN = [E2(nc.sbuf_tensor(f"scn{i}", [128, 512], F32)) for i in range(2)]
                CAR = E2(nc.sbuf_tensor("car", [128, NET], F32))
                pl.phase = "gru.scan"
                for et in range(NET):
                    wstate["whg"][et % 2] = dma(
                        WHGS[et % 2][:], whgi[et],
                        deps=[wstate["whg"][et % 2]], q="pool")
                    tok_w = wstate["whg"][et % 2]
                    car_tok = None
                    for mc in range(4):
                        bh, bgt = PS[2 * (mc % 2)], PS[2 * (mc % 2) + 1]
                        prev_s = pl.now("dve")
                        for kt in range(NKT):
                            mm(bh, bh.t[:], WHGS[et % 2][:, kt * 256:kt * 256 + 128],
                               XNB[kt][:, mc * 512:(mc + 1) * 512], kt == 0, kt == 7,
                               deps=[tok_w, xnb_done])
                        for kt in range(NKT):
                            mm(bgt, bgt.t[:], WHGS[et % 2][:, kt * 256 + 128:kt * 256 + 256],
                               XNB[kt][:, mc * 512:(mc + 1) * 512], kt == 0, kt == 7,
                               deps=[tok_w, xnb_done])
                        tk_c = rd("act", bgt, lambda e, b=bgt: e.activation(
                            CB[:], b.t[:], AF.Sigmoid, scale=-1.0),
                            deps=[prev_s])
                        tk_z = rd("act", bgt, lambda e, b=bgt: e.activation(
                            ZB[:], b.t[:], AF.Sigmoid), deps=[prev_s])
                        tk_r = rd("act", bh, lambda e, b=bh: e.activation(
                            RB[:], b.t[:], AF.Relu), deps=[prev_s])
                        tk_v = pl.op("dve", lambda e: e.scalar_tensor_tensor(
                            VB[:], RB[:], 0.5, ZB[:], ALU.add, ALU.mult),
                            deps=[tk_z, tk_r], tick=True)
                        init_ap = ZCOL[:] if mc == 0 else CAR[:, et:et + 1]
                        tk_s = pl.op("dve", lambda e, i=init_ap, mc=mc:
                                     e.tensor_tensor_scan(
                            SCN[mc % 2][:], CB[:], VB[:], i, ALU.mult, ALU.add),
                            deps=[tk_c, tk_v, car_tok], tick=True)
                        car_tok = pl.op("act", lambda e, et=et, mc=mc: e.copy(
                            CAR[:, et:et + 1], SCN[mc % 2][:, 511:512]),
                            deps=[tk_s], tick=True)
                        pl.op("dve", lambda e, et=et, mc=mc: e.tensor_copy(
                            H[:, et * S + mc * 512:et * S + (mc + 1) * 512],
                            SCN[mc % 2][:]), tick=True)
                    wstate["whg"][et % 2] = pl.now("pe")
                h_done = pl.now("dve")

            with contextlib.ExitStack() as ph3:
                E3 = ph3.enter_context
                pl.phase = "combine"
                pl.fence()
                XTM2 = [E3(nc.sbuf_tensor(f"cxt{i}", [128, D], F32)) for i in range(2)]
                XFM = [E3(nc.sbuf_tensor(f"xfm{d}", [128, 512], F32)) for d in range(NKT)]
                OTM = E3(nc.sbuf_tensor("otm", [128, 4 * 1024], F32))
                WOS = [E3(nc.sbuf_tensor(f"wos{i}", [128, NET * 128], BF16))
                       for i in range(2)]
                OFM = [E3(nc.sbuf_tensor(f"ofm{i}", [128, 512], F32)) for i in range(2)]
                xt_used = [None, None]
                otm_used = None
                xfm_used = None
                ofm_used = [None, None]
                for mc in range(4):
                    xfm_toks = []
                    for i in range(4):
                        t = mc * 4 + i
                        tk_l = dma(XTM2[i % 2][:], xin_t[t], deps=[xt_used[i % 2]])
                        for d in range(NKT):
                            bk = PS[2 + (d % 2)]
                            transp(bk, bk.t[:, 0:128],
                                   XTM2[i % 2][:, d * 128:(d + 1) * 128], IDF[:],
                                   deps=[tk_l])
                            xfm_toks.append(rd("act", bk, lambda e, d=d, i=i, b=bk:
                                               e.copy(XFM[d][:, i * 128:(i + 1) * 128],
                                                      b.t[:, 0:128]),
                                               deps=[xfm_used]))
                        xt_used[i % 2] = pl.now("pe")
                    pend_o = {}

                    def comb_trans(m):
                        tk_o = pend_o[m]
                        for tb in range(4):
                            bt = PS[2 + (tb % 2)]
                            ofm_used[m % 2] = transp(
                                bt, bt.t[:, 0:128],
                                OFM[m % 2][:, tb * 128:(tb + 1) * 128], IDF[:],
                                deps=[tk_o])
                            rd("act", bt, lambda e, b=bt, tb=tb, m=m: e.copy(
                                OTM[:, tb * 1024 + m * 128:tb * 1024 + (m + 1) * 128],
                                b.t[:, 0:128]), deps=[otm_used])

                    for m in range(8):
                        wstate["wos"][m % 2] = dma(
                            WOS[m % 2][:], wout[m],
                            deps=[wstate["wos"][m % 2]], q="pool")
                        tok_w = wstate["wos"][m % 2]
                        bk = PS[m % 2]
                        for kt in range(NET):
                            mm(bk, bk.t[:], WOS[m % 2][:, kt * 128:(kt + 1) * 128],
                               H[:, kt * S + mc * 512:kt * S + (mc + 1) * 512],
                               kt == 0, kt == NET - 1, deps=[tok_w, h_done])
                        tk_t = rd("dve", bk, lambda e, b=bk, m=m, mc=mc:
                                  e.scalar_tensor_tensor(
                            OFM[m % 2][:], b.t[:], 1.0,
                            X[m][:, mc * 512 + 1:mc * 512 + 513],
                            ALU.mult, ALU.mult),
                            deps=silu_toks + [ofm_used[m % 2]])
                        pend_o[m] = pl.op("dve", lambda e, m=m: e.tensor_tensor(
                            OFM[m % 2][:], OFM[m % 2][:], XFM[m][:], ALU.add),
                            deps=xfm_toks, tick=True)
                        xfm_used = pend_o[m]
                        if m > 0:
                            comb_trans(m - 1)
                        wstate["wos"][m % 2] = pl.now("pe")
                    comb_trans(7)
                    act_now = pl.now("act")
                    st_toks = []
                    for i in range(4):
                        st_toks.append(dma(out_t[mc * 4 + i],
                                           OTM[:, i * 1024:(i + 1) * 1024],
                                           deps=[act_now]))
                    otm_used = st_toks[-1]
                    finish_deps.extend(st_toks)

    phase0()
    if stop_after == "p0":
        if "p0" in taps:
            tp = nc.dram_tensor("tap_p0", [NKT, 128, S + 1], F32,
                                kind="ExternalOutput").ap()
            deps = [pl.now(s) for s in ("pe", "dve", "act")]
            for d in range(NKT):
                finish_deps.append(dma(tp[d], X[d][:, 0:S + 1], deps=deps))
        _finish(nc, pl)
        return nc

    nlayers = DEPTH6
    if stop_after and stop_after.startswith("layer"):
        nlayers = int(stop_after[5:]) + 1
    half_layer = stop_after == "attn0"
    substop = None
    if stop_after and stop_after.startswith("attnsub:"):
        substop = stop_after.split(":")[1]
        half_layer = True
        nlayers = 1
    for l in range(nlayers):
        attn_phase(l, substop=substop)
        if half_layer and l == 0:
            break
        ffn_phase(l)

    if stop_after and (stop_after.startswith("layer") or half_layer):
        taps = set(taps)
        if "xstream" in taps:
            tp = nc.dram_tensor("tap_x", [NKT, 128, S + 1], F32,
                                kind="ExternalOutput").ap()
            deps = [pl.now(s) for s in ("pe", "dve", "act")]
            for d in range(NKT):
                finish_deps.append(dma(tp[d], X[d][:, 0:S + 1], deps=deps))
        _finish(nc, pl)
        return nc

    lyr_scope.close()
    dve_done = pl.now("dve")
    silu_toks = [pl.op("act", lambda e, d=d: e.activation(
        X[d][:, 1:S + 1], X[d][:, 1:S + 1], AF.Silu), deps=[dve_done], tick=True)
        for d in range(NKT)]
    mingru(silu_toks)
    _finish(nc, pl)
    return nc


def _finish(nc, pl):
    finals = [(s, pl.cnt[s]) for s in QSEMS["sp"] if pl.cnt[s] > 0]
    sems = pl.sems
    pl.ops["sp"].append(
        lambda e, finals=tuple(finals): [e.wait_ge(sems[s], c) for s, c in finals])
    with nc.Block() as block:
        block.tensor(lambda e: [f(e) for f in pl.ops["pe"]])
        block.vector(lambda e: [f(e) for f in pl.ops["dve"]])
        block.scalar(lambda e: [f(e) for f in pl.ops["act"]])
        block.gpsimd(lambda e: [f(e) for f in pl.ops["pool"]])
        block.sync(lambda e: [f(e) for f in pl.ops["sp"]])


def make_inmaps(inputs):
    x = np.asarray(inputs["L0_attn_x"])
    pw = prep_weights(inputs)
    hc = host_consts()
    shared = {
        "wqkv": np.ascontiguousarray(inputs["L0_ffn_Wqkv"]),
        "wvh_d": pw["wvh"], "wob_d": pw["wob"],
        "w1ag": pw["w1ag"], "w2p": pw["w2p"], "whgi": pw["whgi"],
        "wout": pw["wout"],
        "b1a_cols": pw["b1a_cols"], "b1g_cols": pw["b1g_cols"],
        "lng_cols": pw["lng_cols"], "lnb_cols": pw["lnb_cols"],
        "b2_cols": pw["b2_cols"], "qs_cols": pw["qs_cols"],
        "ks_cols": pw["ks_cols"],
    }
    shared.update(hc)
    in_maps = []
    for b in range(8):
        m = dict(shared)
        m["x"] = np.ascontiguousarray(x[b])
        in_maps.append(m)
    return in_maps


def kernel(**inputs):
    in_maps = make_inmaps(inputs)
    nc = build()
    res = run_bass_kernel_spmd(nc, in_maps, core_ids=list(range(8)))
    out = np.stack([np.asarray(r["out"], np.float32) for r in res.results], 0)
    return out


# version 16
# speedup vs baseline: 1.2995x; 1.2995x over previous
import contextlib
import numpy as np
import ml_dtypes
import concourse.bass as bass
import concourse.mybir as mybir
from concourse.bass_utils import run_bass_kernel_spmd

F32 = mybir.dt.float32
BF16 = mybir.dt.bfloat16
AF = mybir.ActivationFunctionType
ALU = mybir.AluOpType
AX = mybir.AxisListType
BF = ml_dtypes.bfloat16

D = 1024
S = 2048
NKT = 8
NTT = 16
DH = 64
DIM_INNER = 2730
PAD_INNER = 2816
NJ = 22
DEPTH6 = 6
E_GRU = 2048
NET = 16
NEG = -1.0e38

ENGINES = ("pe", "dve", "act", "pool", "sp")
ESEM = dict(pe="pe", dve="dve", act="act", pool="pool", sp="sp_")
NQ = 6
QSEMS = dict(pool=[f"wq{i}" for i in range(NQ)],
             sp=[f"xq{i}" for i in range(NQ)])
ALL_SEMS = ["pe", "dve", "act", "pool"] + QSEMS["pool"] + QSEMS["sp"]


class PL:

    def __init__(self, sems):
        self.phase = ""
        self.labels = {e: [] for e in ENGINES}
        self.ops = {e: [] for e in ENGINES}
        self.sems = sems
        self.cnt = {s: 0 for s in sems}
        self.hwm = {e: {s: 0 for s in sems} for e in ENGINES}
        self.pending = {e: [] for e in ENGINES}
        self.qrot = {q: 0 for q in QSEMS}

    def fence(self, gate_pe=False):
        toks = [self.now(s) for s in ALL_SEMS if s != "pool"]
        for e in ENGINES:
            if e == "pe" and not gate_pe:
                continue
            self.pending[e] = list(toks)

    def op(self, eng, fn, deps=(), tick=False, dma=False, force=()):
        own = ESEM[eng]
        qsems = QSEMS.get(eng, ())
        waits = []
        force = list(force)
        tname, amt = (None, 0)
        if dma:
            tname = qsems[self.qrot[eng] % NQ]
            self.qrot[eng] += 1
            amt = 16
            if self.cnt[tname] > 0:
                force.append((tname, self.cnt[tname]))
        elif tick:
            tname = own
            amt = 1
        for dd in force:
            if dd is None:
                continue
            s, c = dd
            assert c <= self.cnt[s], (f"future wait: {eng} {s} {c} > {self.cnt[s]}")
            if c > self.hwm[eng][s]:
                self.hwm[eng][s] = c
                waits.append((s, c))
        alldeps = list(deps) + self.pending[eng]
        self.pending[eng] = []
        for dd in alldeps:
            if dd is None:
                continue
            s, c = dd
            assert c <= self.cnt[s], (f"future wait: {eng} {s} {c} > {self.cnt[s]}")
            if s == own or (s in qsems and not dma):
                continue
            if c > self.hwm[eng][s]:
                self.hwm[eng][s] = c
                waits.append((s, c))
        tok = None
        if tname is not None:
            self.cnt[tname] += amt
            tok = (tname, self.cnt[tname])
        sems = self.sems

        def run(e, waits=tuple(waits), fn=fn, tname=tname, amt=amt):
            for s, c in waits:
                e.wait_ge(sems[s], c)
            ins = fn(e)
            if tname is not None:
                ins.then_inc(sems[tname], amt)

        self.ops[eng].append(run)
        self.labels[eng].append((self.phase, getattr(self, "phase_shape", None)))
        return tok

    def now(self, sem):
        return (sem, self.cnt[sem])


class Bank:
    def __init__(self, tensor):
        self.t = tensor
        self.w = None
        self.r = []


def host_consts():
    c = {}
    c["identb"] = np.eye(128, dtype=BF)
    c["identf"] = np.eye(128, dtype=np.float32)
    rot = np.zeros((128, 128), np.float32)
    for o in (0, 64):
        for i in range(32):
            rot[o + 32 + i, o + i] = -1.0
            rot[o + i, o + 32 + i] = 1.0
    c["rotp"] = rot.astype(BF)
    sel = np.zeros((8, 128, 16), np.float32)
    selt = np.zeros((8, 16, 128), np.float32)
    for t in range(8):
        for p in range(128):
            h = 2 * t + p // 64
            sel[t, p, h] = 1.0
            selt[t, h, p] = 1.0
    c["sel"] = sel.astype(BF)
    c["selt"] = selt.astype(BF)
    c["onescol"] = np.ones((128, 1), BF)
    c["onesrow"] = np.ones((1, 128), BF)
    i = np.arange(128)[:, None]
    w = np.arange(384)[None, :]
    c["mask_mid"] = np.where((w >= i) & (w <= i + 256), 0.0, NEG).astype(BF)
    w0 = np.arange(128)[None, :]
    c["mask_t0"] = np.where(w0 <= i, 0.0, NEG).astype(BF)
    w1 = np.arange(256)[None, :]
    c["mask_t1"] = np.where(w1 <= i + 128, 0.0, NEG).astype(BF)
    half = np.arange(0, DH, 2, dtype=np.float64)
    inv_freq = 1.0 / 10000.0 ** (half / DH)
    t = np.arange(S, dtype=np.float64)
    freqs = t[:, None] * inv_freq[None, :]
    freqs = np.concatenate([freqs, freqs], -1)
    base_scale = (half + 0.4 * DH) / (1.4 * DH)
    power = (t - S // 2) / (256 // 2)
    scale = base_scale[None, :] ** power[:, None]
    scale = np.concatenate([scale, scale], -1)
    cosf, sinf = np.cos(freqs), np.sin(freqs)
    for name, tab in [("cq", cosf * scale), ("sq", sinf * scale),
                      ("ck", cosf / scale), ("sk", sinf / scale)]:
        fm = tab.T.astype(BF)
        c[name] = np.concatenate([fm, fm], 0)
    return c


def prep_weights(inputs):
    d = {}
    W1 = np.asarray(inputs["L0_ffn_W1"])
    a = np.zeros((DEPTH6, D, PAD_INNER), np.float32)
    g = np.zeros((DEPTH6, D, PAD_INNER), np.float32)
    a[:, :, :DIM_INNER] = W1[:, :, :DIM_INNER]
    g[:, :, :DIM_INNER] = W1[:, :, DIM_INNER:]
    w1ag = np.empty((DEPTH6, D, NJ * 2, 128), np.float32)
    w1ag[:, :, 0::2, :] = a.reshape(DEPTH6, D, NJ, 128)
    w1ag[:, :, 1::2, :] = g.reshape(DEPTH6, D, NJ, 128)
    w1v = w1ag.reshape(DEPTH6, NKT, 128, NJ, 256)
    d["w1ag"] = np.ascontiguousarray(w1v.transpose(0, 3, 2, 1, 4).reshape(
        DEPTH6, NJ, 128, NKT * 256))
    W2 = np.asarray(inputs["L0_ffn_W2"])
    w2p = np.zeros((DEPTH6, PAD_INNER, D), np.float32)
    w2p[:, :DIM_INNER] = W2
    w2v = w2p.reshape(DEPTH6, NJ, 128, 8, 128)
    d["w2p"] = np.ascontiguousarray(w2v.transpose(0, 3, 2, 1, 4).reshape(
        DEPTH6, 8, 128, NJ * 128))
    b1 = np.asarray(inputs["L0_ffn_b1"])
    b1a = np.zeros((DEPTH6, PAD_INNER), np.float32)
    b1g = np.zeros((DEPTH6, PAD_INNER), np.float32)
    b1a[:, :DIM_INNER] = b1[:, :DIM_INNER]
    b1g[:, :DIM_INNER] = b1[:, DIM_INNER:]
    d["b1a_cols"] = np.ascontiguousarray(
        b1a.reshape(DEPTH6, NJ, 128).transpose(0, 2, 1))
    d["b1g_cols"] = np.ascontiguousarray(
        b1g.reshape(DEPTH6, NJ, 128).transpose(0, 2, 1))
    for name, key in [("lng", "L0_ffn_ln_g"), ("lnb", "L0_ffn_ln_b"),
                      ("b2", "L0_ffn_b2")]:
        v = np.asarray(inputs[key]).reshape(DEPTH6, 8, 128)
        d[name + "_cols"] = np.ascontiguousarray(v.transpose(0, 2, 1))
    qs = np.asarray(inputs["L0_ffn_q_scale"])
    ks = np.asarray(inputs["L0_ffn_k_scale"])
    d["qs_cols"] = np.ascontiguousarray(np.concatenate([qs, qs], 1).T)
    d["ks_cols"] = np.ascontiguousarray(np.concatenate([ks, ks], 1).T)
    Whg = np.asarray(inputs["L0_attn_W_hg"])
    whgi = np.empty((D, NET, 256), np.float32)
    whgi[:, :, 0:128] = Whg[:, 0:E_GRU].reshape(D, NET, 128)
    whgi[:, :, 128:256] = Whg[:, E_GRU:].reshape(D, NET, 128)
    whv = whgi.reshape(NKT, 128, NET, 256)
    d["whgi"] = np.ascontiguousarray(whv.transpose(2, 1, 0, 3).reshape(
        NET, 128, NKT * 256))
    wo = np.asarray(inputs["L0_attn_W_out"])
    wov = wo.reshape(NET, 128, 8, 128)
    d["wout"] = np.ascontiguousarray(wov.transpose(2, 1, 0, 3).reshape(
        8, 128, NET * 128))
    wq = np.asarray(inputs["L0_ffn_Wqkv"])
    wv = wq[:, :, 2048:3072].reshape(DEPTH6, NKT, 128, 2, 512)
    d["wvh"] = np.ascontiguousarray(wv.transpose(0, 3, 2, 1, 4).reshape(
        DEPTH6, 2, 128, NKT * 512))
    wo2 = np.asarray(inputs["L0_ffn_Wo"]).reshape(DEPTH6, NKT, 128, 2, 512)
    d["wob"] = np.ascontiguousarray(wo2.transpose(0, 3, 2, 1, 4).reshape(
        DEPTH6, 2, 128, NKT * 512))
    return d


def build(stop_after=None, taps=()):
    nc = bass.Bass("TRN2", target_bir_lowering=False)
    taps = set(taps)

    x_in = nc.dram_tensor("x", [S, D], F32, kind="ExternalInput").ap()
    wqkv = nc.dram_tensor("wqkv", [DEPTH6, D, 3072], F32, kind="ExternalInput").ap()
    w1ag = nc.dram_tensor("w1ag", [DEPTH6, NJ, 128, NKT * 256], F32, kind="ExternalInput").ap()
    w2p = nc.dram_tensor("w2p", [DEPTH6, 8, 128, NJ * 128], F32, kind="ExternalInput").ap()
    whgi = nc.dram_tensor("whgi", [NET, 128, NKT * 256], F32, kind="ExternalInput").ap()
    wout = nc.dram_tensor("wout", [8, 128, NET * 128], F32, kind="ExternalInput").ap()
    b1a_c = nc.dram_tensor("b1a_cols", [DEPTH6, 128, NJ], F32, kind="ExternalInput").ap()
    b1g_c = nc.dram_tensor("b1g_cols", [DEPTH6, 128, NJ], F32, kind="ExternalInput").ap()
    lng_c = nc.dram_tensor("lng_cols", [DEPTH6, 128, 8], F32, kind="ExternalInput").ap()
    lnb_c = nc.dram_tensor("lnb_cols", [DEPTH6, 128, 8], F32, kind="ExternalInput").ap()
    b2_c = nc.dram_tensor("b2_cols", [DEPTH6, 128, 8], F32, kind="ExternalInput").ap()
    qs_c = nc.dram_tensor("qs_cols", [128, DEPTH6], F32, kind="ExternalInput").ap()
    ks_c = nc.dram_tensor("ks_cols", [128, DEPTH6], F32, kind="ExternalInput").ap()
    wvh_d = nc.dram_tensor("wvh_d", [DEPTH6, 2, 128, NKT * 512], F32, kind="ExternalInput").ap()
    wob_d = nc.dram_tensor("wob_d", [DEPTH6, 2, 128, NKT * 512], F32, kind="ExternalInput").ap()
    cst = {}
    for name, shape, dt in [
        ("identb", [128, 128], BF16), ("identf", [128, 128], F32),
        ("rotp", [128, 128], BF16), ("sel", [8, 128, 16], BF16),
        ("selt", [8, 16, 128], BF16), ("onescol", [128, 1], BF16),
        ("onesrow", [1, 128], BF16), ("mask_mid", [128, 384], BF16),
        ("mask_t0", [128, 128], BF16), ("mask_t1", [128, 256], BF16),
        ("cq", [128, S], BF16), ("sq", [128, S], BF16),
        ("ck", [128, S], BF16), ("sk", [128, S], BF16),
    ]:
        cst[name] = nc.dram_tensor(name, shape, dt, kind="ExternalInput").ap()
    out_d = nc.dram_tensor("out", [S, D], F32, kind="ExternalOutput").ap()
    out_t = out_d.rearrange("(t p) d -> t p d", p=128)
    xin_t = x_in.rearrange("(t p) d -> t p d", p=128)
    wqkv_v = wqkv.rearrange("l (kt p) e -> l p kt e", p=128)

    top = contextlib.ExitStack()
    E = top.enter_context
    sems = {s: E(nc.semaphore(s)) for s in ALL_SEMS}
    pl = PL(sems)

    X = [E(nc.sbuf_tensor(f"X{d}", [128, S + 4], F32)) for d in range(NKT)]
    IDB = E(nc.sbuf_tensor("s_identb", [128, 128], BF16))
    IDF = E(nc.sbuf_tensor("s_identf", [128, 128], F32))
    ROTP = E(nc.sbuf_tensor("s_rotp", [128, 128], BF16))
    SEL = [E(nc.sbuf_tensor(f"s_sel{t}", [128, 16], BF16)) for t in range(8)]
    SELT = [E(nc.sbuf_tensor(f"s_selt{t}", [16, 128], BF16)) for t in range(8)]
    OC = E(nc.sbuf_tensor("s_onescol", [128, 1], BF16))
    OR = E(nc.sbuf_tensor("s_onesrow", [1, 128], BF16))
    NMU = E(nc.sbuf_tensor("negmu", [128, NTT], F32))
    RSD = E(nc.sbuf_tensor("rstd", [128, NTT], F32))
    QS8 = E(nc.sbuf_tensor("qs8", [128, DEPTH6], F32))
    KS1 = E(nc.sbuf_tensor("ks1", [128, DEPTH6], F32))
    HSTA = E(nc.sbuf_tensor("hst_a", [128, 3 * NKT], F32))
    HSTF = E(nc.sbuf_tensor("hst_f", [128, NKT], F32))
    ZCOL = E(nc.sbuf_tensor("zcol", [128, 1], F32))
    EPSC = E(nc.sbuf_tensor("epsc", [128, 1], F32))
    lyr_scope = contextlib.ExitStack()
    EL_ = lyr_scope.enter_context
    MMK = EL_(nc.sbuf_tensor("maskm", [128, 384], BF16))
    TAB = {k: EL_(nc.sbuf_tensor("tab_" + k, [128, S], BF16))
           for k in ("cq", "sq", "ck", "sk")}

    PS = [Bank(E(nc.psum_tensor(f"ps{i}", [128, 512], F32))) for i in range(6)]
    PTB2 = [Bank(E(nc.psum_tensor(f"ptb{i}", [128, 768], BF16))) for i in range(2)]
    PTB = PTB2[0]
    PV0 = Bank(PS[0].t)
    PV1 = Bank(PS[1].t)

    def dma(dst_ap, src_ap, deps=(), q="sp"):
        return pl.op(q, lambda e, d=dst_ap, s=src_ap: e.dma_start(out=d, in_=s),
                     deps=deps, dma=True)

    def mm(bank, out_ap, lh, rh, start, stop, deps=(), tick=False):
        deps = list(deps)
        if start:
            deps += bank.r
            bank.r = []
        pl.phase_shape = f"{lh.shape[0]}*{lh.shape[-1]}*{rh.shape[-1]}"
        tok = pl.op("pe", lambda e, o=out_ap, a=lh, b=rh, st=start, sp=stop:
                    e.matmul(o, a, b, start=st, stop=sp), deps=deps,
                    tick=(tick or stop))
        pl.phase_shape = None
        if stop:
            bank.w = tok
        return tok

    def transp(bank, out_ap, in_ap, ident, deps=(), first=True, last=True):
        deps = list(deps)
        if first:
            deps += bank.r
            bank.r = []
        pl.phase_shape = "T"
        tok = pl.op("pe", lambda e, o=out_ap, i=in_ap, d=ident:
                    e.transpose(o, i, d), deps=deps, tick=True)
        pl.phase_shape = None
        if last:
            bank.w = tok
        return tok

    def rd(eng, bank, fn, deps=(), tick=True, force=()):
        tok = pl.op(eng, fn, deps=list(deps) + [bank.w], tick=tick, force=force)
        bank.r.append(tok)
        return tok

    for name, buf in [("mask_mid", MMK),
                      ("identb", IDB), ("identf", IDF), ("rotp", ROTP),
                      ("onescol", OC), ("onesrow", OR)]:
        dma(buf[:], cst[name])
    for k in TAB:
        dma(TAB[k][:], cst[k])
    for t in range(8):
        dma(SEL[t][:], cst["sel"][t])
        dma(SELT[t][:], cst["selt"][t])
    dma(QS8[:], qs_c)
    dma(KS1[:], ks_c)
    consts_toks = [pl.now(s) for s in QSEMS["sp"]]
    pl.op("dve", lambda e: e.tensor_scalar(QS8[:], QS8[:], 8.0, None, ALU.mult),
          deps=consts_toks, tick=True)
    pl.op("dve", lambda e: e.memset(ZCOL[:], 0.0), tick=True)
    pl.op("dve", lambda e: e.memset(EPSC[:], 1e-5), tick=True)

    finish_deps = []

    def phase0():
        with contextlib.ExitStack() as ph:
            E0 = ph.enter_context
            pl.phase = "p0"
            pl.fence()
            XT = [E0(nc.sbuf_tensor(f"xt{t}", [128, D], F32)) for t in range(NTT)]
            SQ0 = E0(nc.sbuf_tensor("sq0", [128, D], F32))
            CLS = E0(nc.sbuf_tensor("p0cols", [128, 8], F32))
            XNT = E0(nc.sbuf_tensor("xnt", [128, D], F32))
            tok_x = [dma(XT[t][:], xin_t[t]) for t in range(NTT)]
            stat_rs = []
            tk_ln = None
            tk_red = None
            for t in range(NTT):
                tk_r0 = pl.op("dve", lambda e, t=t: e.reduce_sum(
                    CLS[:, 0:1], XT[t][:], AX.X), deps=[tok_x[t]], tick=True)
                tk_sq = pl.op("act", lambda e, t=t: e.activation(
                    SQ0[:], XT[t][:], AF.Square), deps=[tok_x[t], tk_red], tick=True)
                tk_red = pl.op("dve", lambda e: e.reduce_sum(CLS[:, 1:2], SQ0[:], AX.X),
                               deps=[tk_sq], tick=True)
                pl.op("dve", lambda e, t=t: e.tensor_scalar(
                    NMU[:, t:t + 1], CLS[:, 0:1], -1.0 / D, None, ALU.mult),
                    force=[tk_r0])
                tk_m2 = pl.op("dve", lambda e: e.tensor_tensor(
                    CLS[:, 2:3], CLS[:, 0:1], CLS[:, 0:1], ALU.mult),
                    force=[tk_r0], tick=True)
                tk_m3 = pl.op("dve", lambda e: e.tensor_scalar(
                    CLS[:, 3:4], CLS[:, 2:3], 1.0 / (float(D) * D), None, ALU.mult),
                    force=[tk_m2], tick=True)
                tk_var = pl.op("dve", lambda e: e.scalar_tensor_tensor(
                    CLS[:, 4:5], CLS[:, 1:2], 1.0 / D, CLS[:, 3:4],
                    ALU.mult, ALU.subtract), deps=[tk_ln],
                    force=[tk_m3, tk_red], tick=True)
                tk_ln = pl.op("act", lambda e: e.activation(
                    CLS[:, 5:6], CLS[:, 4:5], AF.Ln, bias=EPSC[:], scale=1.0),
                    deps=[tk_var], tick=True)
                tk_rs = pl.op("act", lambda e, t=t: e.activation(
                    RSD[:, t:t + 1], CLS[:, 5:6], AF.Exp, scale=-0.5),
                    deps=[tk_ln], force=[tk_ln], tick=True)
                stat_rs.append(tk_rs)
            for d in range(NKT):
                pl.op("dve", lambda e, d=d: e.memset(X[d][:, 0:1], 0.0))
            xnt_used = None
            for t in range(NTT):
                tk_n = pl.op("dve", lambda e, t=t: e.tensor_scalar(
                    XNT[:], XT[t][:], NMU[:, t:t + 1], RSD[:, t:t + 1],
                    ALU.add, ALU.mult), deps=[stat_rs[t], xnt_used], tick=True)
                for d in range(NKT):
                    bk = PS[d % 2]
                    xnt_used = transp(bk, bk.t[:, 0:128],
                                      XNT[:, d * 128:(d + 1) * 128], IDF[:],
                                      deps=[tk_n])
                    rd("act", bk, lambda e, b=bk, t=t, d=d: e.copy(
                        X[d][:, 1 + t * 128: 1 + (t + 1) * 128], b.t[:, 0:128]))

    wstate = dict(wqk=None, wvh=[None, None], wob=[None, None],
                  w1=[None, None, None, None], w2=[None, None],
                  whg=[None, None], wos=[None, None])

    def attn_phase(l, substop=None):
        with contextlib.ExitStack() as ph:
            EA = ph.enter_context
            pl.fence()
            XB = [EA(nc.sbuf_tensor(f"axb{d}_l{l}", [128, 516], BF16)) for d in range(NKT)]
            WQK = EA(nc.sbuf_tensor(f"wqk_l{l}", [128, 8 * 2048], BF16))
            WVH = [EA(nc.sbuf_tensor(f"wvh{i}_l{l}", [128, 8 * 512], BF16))
                   for i in range(2)]
            WOB = [EA(nc.sbuf_tensor(f"wob{i}_l{l}", [128, 8 * 512], BF16))
                   for i in range(2)]
            QR = [EA(nc.sbuf_tensor(f"qr{m}_l{l}", [128, 512], BF16)) for m in range(8)]
            KR = [EA(nc.sbuf_tensor(f"kr{m}_l{l}", [128, 768], BF16)) for m in range(8)]
            VR = [EA(nc.sbuf_tensor(f"vrr{s}_l{l}", [128, 1024], BF16)) for s in range(6)]
            AC = [EA(nc.sbuf_tensor(f"acc{d}_l{l}", [128, 512], BF16)) for d in range(NKT)]
            SSF = EA(nc.sbuf_tensor(f"ssf_l{l}", [16, 512], F32))
            NIV = EA(nc.sbuf_tensor(f"niv_l{l}", [16, 512], BF16))
            PN = [EA(nc.sbuf_tensor(f"pn{i}_l{l}", [128, 768], BF16))
                  for i in range(2)]
            PTS = [EA(nc.sbuf_tensor(f"pts{i}_l{l}", [128, 768], BF16))
                   for i in range(2)]
            NML = EA(nc.sbuf_tensor(f"nml_l{l}", [128, 16], F32))
            RCL = EA(nc.sbuf_tensor(f"rcl_l{l}", [128, 4], F32))

            wqk_src = wqkv_v[l][:, :, 0:2048]
            wqk_dst = WQK[:].rearrange("p (kt e) -> p kt e", e=2048)
            wstate["wqk"] = dma(wqk_dst, wqk_src, deps=[wstate["wqk"]], q="pool")
            tok_wqk = wstate["wqk"]

            dve_now = pl.now("dve")
            stash = []
            for d in range(NKT):
                for i in range(3):
                    stash.append(pl.op("act", lambda e, d=d, i=i: e.copy(
                        HSTA[:, 3 * d + i:3 * d + i + 1],
                        X[d][:, (i + 1) * 512:(i + 1) * 512 + 1]),
                        deps=[dve_now], tick=True))

            for c in range(4):
                if c > 0:
                    pl.fence()
                base = c * 512
                xb_toks = []
                for d in range(NKT):
                    if c == 0:
                        pl.op("dve", lambda e, d=d: e.tensor_copy(
                            XB[d][:, 0:1], X[d][:, 0:1]))
                    else:
                        pl.op("dve", lambda e, d=d, c=c: e.tensor_copy(
                            XB[d][:, 0:1], HSTA[:, 3 * d + c - 1:3 * d + c]),
                            deps=[stash[3 * d + c - 1]])
                    xb_toks.append(pl.op("dve", lambda e, d=d, base=base: e.tensor_copy(
                        XB[d][:, 1:513], X[d][:, base + 1:base + 513]), tick=True))

                def xs_ap(kt, lo, hi):
                    off = 0 if kt >= 4 else 1
                    return XB[kt][:, lo + off:hi + off]

                if substop == "xb":
                    return

                if c > 0:
                    for m in range(8):
                        pl.op("dve", lambda e, m=m: e.tensor_copy(
                            KR[m][:, 0:256], KR[m][:, 512:768]), tick=(m == 7))
                    for s in range(2):
                        pl.op("dve", lambda e, s=s: e.tensor_copy(
                            VR[s][:], VR[4 + s][:]), tick=(s == 1))
                roll_tok = pl.now("dve")

                pl.phase = f"attn.prep"
                qsq_used = [None, None]
                perm_used = {}
                sq_done = {}
                PARTS = [(0, QS8, "cq", "sq"), (1024, KS1, "ck", "sk")]
                cp_toks = {}

                def raw_ap(part, m):
                    return QR[m][:, 0:512] if part == 0 else KR[m][:, 256:768]

                for part, (eoff, scl, tabc, tabs_) in enumerate(PARTS):
                    for m in range(8):
                        bk = PS[m % 4]
                        for kt in range(NKT):
                            mm(bk, bk.t[:], WQK[:, kt * 2048 + eoff + m * 128:
                                                 kt * 2048 + eoff + (m + 1) * 128],
                               xs_ap(kt, 0, 512), kt == 0, kt == 7,
                               deps=[tok_wqk, xb_toks[kt]])
                        cp_toks[(part, m)] = rd(
                            "dve", bk, lambda e, b=bk, a=raw_ap(part, m):
                            e.tensor_copy(a, b.t[:]),
                            deps=([roll_tok] if part == 1 else []))

                def sumsq_group(part):
                    for m in range(8):
                        tk_sq = pl.op("act", lambda e, a=raw_ap(part, m), m=m:
                                      e.activation(PN[m % 2][:, 0:512], a, AF.Square),
                                      deps=[cp_toks[(part, m)], qsq_used[m % 2]],
                                      tick=True)
                        sq_done[(part, m)] = tk_sq
                        qsq_used[m % 2] = mm(
                            PS[4], PS[4].t[0:16, 0:512], SEL[m][:],
                            PN[m % 2][:, 0:512], m == 0, m == 7, deps=[tk_sq],
                            tick=True)
                    tk_ssf = rd("dve", PS[4], lambda e: e.tensor_copy(
                        SSF[:], PS[4].t[0:16, 0:512]))
                    tk_mx = pl.op("dve", lambda e: e.tensor_scalar(
                        SSF[:], SSF[:], 1e-24, None, ALU.max), force=[tk_ssf],
                        tick=True)
                    tk_ln = pl.op("act", lambda e: e.activation(
                        SSF[:], SSF[:], AF.Ln), deps=[tk_mx], tick=True)
                    return pl.op("act", lambda e: e.activation(
                        NIV[:], SSF[:], AF.Exp, scale=-0.5), deps=[tk_ln],
                        force=[tk_ln], tick=True)

                def v_group_tt(eh, tt, tok_wv):
                    bk = PS[tt % 2]
                    for kt in range(NKT):
                        off = 0 if kt >= 4 else 1
                        mm(bk, bk.t[:], XB[kt][:, tt * 128 + off:tt * 128 + off + 128],
                           WVH[eh][:, kt * 512:(kt + 1) * 512], kt == 0, kt == 7,
                           deps=[tok_wv, xb_toks[kt], roll_tok])
                    rd("dve", bk, lambda e, b=bk, tt=tt, eh=eh: e.tensor_copy(
                        VR[2 + tt][:, eh * 512:(eh + 1) * 512], b.t[:]))

                def scale_rot(part, tk_niv, eh, tok_wv):
                    eoff, scl, tabc, tabs_ = PARTS[part]
                    for m in range(8):
                        a = raw_ap(part, m)
                        bb = PS[4 + (m % 2)] if part == 0 else PS[5 - (m % 2)]
                        mm(bb, bb.t[:], SELT[m][:], NIV[:], True, True,
                           deps=[tk_niv])
                        tk_scale = rd("dve", bb, lambda e, b=bb, a=a, scl=scl, l=l:
                                      e.scalar_tensor_tensor(
                            a, b.t[:], scl[:, l:l + 1], a, ALU.mult, ALU.mult),
                            deps=[sq_done[(part, m)]])
                        rb = PS[2 + (m % 2)]
                        tk_perm = mm(rb, rb.t[:], ROTP[:], a, True, True,
                                     deps=[tk_scale])
                        rd("dve", rb, lambda e, b=rb, tn=tabs_, base=base:
                           e.tensor_tensor(PTS[0][:, 0:512], b.t[:],
                                           TAB[tn][:, base:base + 512], ALU.mult))
                        pl.op("dve", lambda e, a=a, tn=tabc, base=base:
                              e.tensor_tensor(a, a,
                                              TAB[tn][:, base:base + 512], ALU.mult),
                              deps=[tk_perm])
                        pl.op("dve", lambda e, a=a: e.tensor_tensor(
                            a, a, PTS[0][:, 0:512], ALU.add), tick=True)
                        if m in (3, 5, 7):
                            v_group_tt(eh, (m - 1) // 2, tok_wv)

                tok_wv = [None, None]
                for eh in range(2):
                    tok_wv[eh] = dma(WVH[eh][:], wvh_d[l][eh],
                                     deps=[wstate["wvh"][eh]], q="pool")
                tk_nivq = sumsq_group(0)
                v_group_tt(0, 0, tok_wv[0])
                scale_rot(0, tk_nivq, 0, tok_wv[0])
                wstate["wvh"][0] = pl.now("pe")
                tk_nivk = sumsq_group(1)
                v_group_tt(1, 0, tok_wv[1])
                scale_rot(1, tk_nivk, 1, tok_wv[1])
                wstate["wvh"][1] = pl.now("pe")
                qk_done = pl.now("dve")
                v_done = pl.now("dve")
                prep_pe = pl.now("pe")
                if substop == "v":
                    return

                pl.phase = "attn.units"
                units = [(hp, qi) for hp in range(8) for qi in range(4)]

                def unit_geom(qi):
                    if c == 0 and qi < 2:
                        w = 128 * (qi + 1)
                        rc0 = 256
                        mask = MMK[:, 384 - w:384]
                    else:
                        w = 384
                        rc0 = qi * 128
                        mask = MMK[:, 0:384]
                    njb = w // 128
                    return w, rc0, mask, njb, qi + (3 - njb)

                stash_u = {}

                def emit_front(u):
                    hp, qi = units[u]
                    p = u % 2
                    w, rc0, mask, njb, slot0 = unit_geom(qi)
                    sbk = [PS[2 + 2 * p], PS[3 + 2 * p]]
                    sim_toks = []
                    for hi, hb in ((0, 0), (1, 64)):
                        bk = sbk[hi]
                        mm(bk, bk.t[:, 0:w], IDB[:], mask, True, False)
                        sim_toks.append(mm(
                            bk, bk.t[:, 0:w],
                            QR[hp][hb:hb + 64, qi * 128:(qi + 1) * 128],
                            KR[hp][hb:hb + 64, rc0:rc0 + w], False, True,
                            deps=[qk_done]))
                    nm = []
                    for hi in range(2):
                        bk = sbk[hi]
                        nm.append(rd("dve", bk, lambda e, b=bk, hi=hi, w=w, p=p:
                                     e.reduce_max(NML[:, 8 * p + hi:8 * p + hi + 1],
                                                  b.t[:, 0:w], AX.X, negate=True)))
                    exps = []
                    for hi in range(2):
                        bk = sbk[hi]
                        exps.append(rd("act", bk, lambda e, b=bk, hi=hi, w=w, p=p:
                                       e.activation(
                            PN[p][:, hi * 384:hi * 384 + w], b.t[:, 0:w], AF.Exp,
                            bias=NML[:, 8 * p + hi:8 * p + hi + 1],
                            accum_out=NML[:, 8 * p + 2 + hi:8 * p + 3 + hi]),
                            deps=[nm[hi], stash_u.get(u - 2, {}).get("tr_last"),
                                  (prep_pe if u < 2 else None)]))
                    stash_u[u] = dict(exps=exps, njb=njb, w=w,
                                      slot0=slot0, hp=hp, qi=qi, p=p)

                def emit_scale(u):
                    st_ = stash_u[u]
                    p = st_["p"]
                    w = st_["w"]
                    exps = st_["exps"]
                    tk_rcl = pl.op("dve", lambda e, p=p: e.reciprocal(
                        RCL[:, 2 * p:2 * p + 2], NML[:, 8 * p + 2:8 * p + 4]),
                        deps=exps, tick=True)
                    sc_toks = []
                    for hi in range(2):
                        sc_toks.append(pl.op("act", lambda e, hi=hi, w=w, p=p:
                                             e.activation(
                            PN[p][:, hi * 384:hi * 384 + w],
                            PN[p][:, hi * 384:hi * 384 + w], AF.Copy,
                            scale=RCL[:, 2 * p + hi:2 * p + hi + 1]),
                            deps=[exps[hi], tk_rcl], tick=True))
                    st_["sc"] = sc_toks

                def emit_trans(u):
                    st_ = stash_u[u]
                    njb = st_["njb"]
                    p = u % 2
                    ptb = PTB2[p]
                    trs = []
                    first = True
                    for hi in range(2):
                        for jb in range(njb):
                            dep = [st_["sc"][hi]]
                            trs.append(transp(
                                ptb, ptb.t[:, (hi * 3 + jb) * 128:(hi * 3 + jb + 1) * 128],
                                PN[p][:, hi * 384 + jb * 128:hi * 384 + (jb + 1) * 128],
                                IDB[:], deps=dep, first=first,
                                last=(hi == 1 and jb == njb - 1)))
                            first = False
                    st_["tr_last"] = trs[-1]
                    pts_war = stash_u.get(u - 2, {}).get("pv_last")
                    if njb == 3:
                        e0 = rd("dve", ptb, lambda e, p=p: e.tensor_copy(
                            PTS[p][:, 0:768], ptb.t[:, 0:768]), deps=[pts_war])
                        st_["ptse"] = e0
                    else:
                        e0 = rd("dve", ptb, lambda e, njb=njb, p=p: e.tensor_copy(
                            PTS[p][:, 0:njb * 128], ptb.t[:, 0:njb * 128]),
                            deps=[pts_war])
                        st_["ptse"] = rd("dve", ptb, lambda e, njb=njb, p=p:
                                         e.tensor_copy(
                            PTS[p][:, 384:384 + njb * 128],
                            ptb.t[:, 384:384 + njb * 128]))
                    st_["ptse0"] = e0

                def emit_pv(u):
                    st_ = stash_u[u]
                    hp, qi, njb, slot0 = st_["hp"], st_["qi"], st_["njb"], st_["slot0"]
                    p = u % 2
                    pvb = [PV0, PV1][p]
                    for hi, hb in ((0, 0), (1, 64)):
                        for jb in range(njb):
                            st_["pv_last"] = mm(
                               pvb, PS[p].t[hb:hb + 64, 0:128],
                               VR[slot0 + jb][:, (2 * hp + hi) * 64:(2 * hp + hi + 1) * 64],
                               PTS[p][:, hi * 384 + jb * 128:hi * 384 + (jb + 1) * 128],
                               jb == 0, jb == njb - 1,
                               deps=[st_["ptse0"], st_["ptse"], v_done])
                    tk_a = pl.op("act", lambda e, hp=hp, qi=qi, p=p: e.copy(
                        AC[hp][:, qi * 128:(qi + 1) * 128], PS[p].t[:, 0:128]),
                        deps=[pvb.w], tick=True)
                    pvb.r.append(tk_a)
                    stash_u.pop(u - 2, None)

                for u in range(len(units)):
                    emit_front(u)
                    if u > 0:
                        emit_trans(u - 1)
                    if u > 1:
                        emit_pv(u - 2)
                    emit_scale(u)
                emit_trans(len(units) - 1)
                emit_pv(len(units) - 2)
                emit_pv(len(units) - 1)
                a_done = pl.now("act")
                if substop == "units":
                    return

                pl.phase = "attn.wo"
                res_toks = []
                tok_wob = [None, None]
                for hf in range(2):
                    tok_wob[hf] = dma(WOB[hf][:], wob_d[l][hf],
                                      deps=[wstate["wob"][hf]], q="pool")
                for m in range(8):
                    bk = PS[m % 4]
                    for kt in range(NKT):
                        mm(bk, bk.t[:], WOB[m // 4][:, kt * 512 + (m % 4) * 128:
                                                    kt * 512 + (m % 4) * 128 + 128],
                           AC[kt][:, 0:512], kt == 0, kt == 7,
                           deps=[tok_wob[m // 4], a_done])
                    if m == 3:
                        wstate["wob"][0] = pl.now("pe")
                    if m == 7:
                        wstate["wob"][1] = pl.now("pe")
                    res_toks.append(rd("dve", bk, lambda e, b=bk, m=m, base=base:
                                       e.scalar_tensor_tensor(
                        X[m][:, base + 1:base + 513], b.t[:], 1.0,
                        X[m][:, base + 1:base + 513], ALU.mult, ALU.add),
                        deps=stash))
            wstate["wqk"] = pl.now("pe")

    def ffn_phase(l):
        with contextlib.ExitStack() as ph:
            EF = ph.enter_context
            pl.fence()
            XB2 = [EF(nc.sbuf_tensor(f"fxb{d}_l{l}", [128, 1028], BF16)) for d in range(NKT)]
            GAT = EF(nc.sbuf_tensor(f"gated_l{l}", [128, NJ * 1024], BF16))
            SQF = [EF(nc.sbuf_tensor(f"sqf{i}_l{l}", [128, 512], BF16)) for i in range(2)]
            XN2 = [EF(nc.sbuf_tensor(f"xn2_{d}_l{l}", [128, 1024], BF16)) for d in range(NKT)]
            W1S = [EF(nc.sbuf_tensor(f"w1s{i}_l{l}", [128, 8 * 256], BF16)) for i in range(4)]
            W2S = [EF(nc.sbuf_tensor(f"w2s{i}_l{l}", [128, NJ * 128], BF16)) for i in range(2)]
            ROWS = EF(nc.sbuf_tensor(f"rows_l{l}", [1, 5 * 512], F32))
            ROWB = EF(nc.sbuf_tensor(f"rowb_l{l}", [1, 2 * 512], BF16))
            TN = [EF(nc.sbuf_tensor(f"tn{i}_l{l}", [128, 512], F32)) for i in range(2)]
            GG = EF(nc.sbuf_tensor(f"gg_l{l}", [128, 512], BF16))
            B1A = EF(nc.sbuf_tensor(f"b1a_l{l}", [128, NJ], F32))
            B1G = EF(nc.sbuf_tensor(f"b1g_l{l}", [128, NJ], F32))
            LNG = EF(nc.sbuf_tensor(f"lng_l{l}", [128, 8], F32))
            LNB = EF(nc.sbuf_tensor(f"lnb_l{l}", [128, 8], F32))
            B2C = EF(nc.sbuf_tensor(f"b2c_l{l}", [128, 8], F32))
            vec_toks = [dma(B1A[:], b1a_c[l]), dma(B1G[:], b1g_c[l]),
                        dma(LNG[:], lng_c[l]), dma(LNB[:], lnb_c[l]),
                        dma(B2C[:], b2_c[l])]
            pl.pending["dve"].extend(vec_toks)
            pl.pending["act"].extend(vec_toks)

            dve_now = pl.now("dve")
            stash = [pl.op("act", lambda e, d=d: e.copy(
                HSTF[:, d:d + 1], X[d][:, 1024:1025]), deps=[dve_now], tick=True)
                for d in range(NKT)]
            for cc in range(2):
                pl.phase = "ffn.ln"
                if cc > 0:
                    pl.fence()
                fb = cc * 1024
                xb_toks = []
                for d in range(NKT):
                    if cc == 0:
                        pl.op("dve", lambda e, d=d: e.tensor_copy(
                            XB2[d][:, 0:1], X[d][:, 0:1]))
                    else:
                        pl.op("dve", lambda e, d=d: e.tensor_copy(
                            XB2[d][:, 0:1], HSTF[:, d:d + 1]), deps=[stash[d]])
                    xb_toks.append(pl.op("dve", lambda e, d=d, fb=fb: e.tensor_copy(
                        XB2[d][:, 1:1025], X[d][:, fb + 1:fb + 1025]), tick=True))

                def xs2(kt, lo, hi):
                    off = 0 if kt >= 4 else 1
                    return XB2[kt][:, lo + off:hi + off]

                xn_done_sc = [None, None]
                xn_toks = []
                for sc in range(2):
                    scb = sc * 512
                    sq_toks = [None, None]
                    bkx, bkq = PS[2], PS[3]
                    for kt in range(NKT):
                        mm(bkx, bkx.t[0:1, 0:512], OC[:], xs2(kt, scb, scb + 512),
                           kt == 0, kt == 7, deps=[xb_toks[kt]], tick=(kt == 7))
                    for kt in range(NKT):
                        tk_s = pl.op("act", lambda e, kt=kt, scb=scb: e.activation(
                            SQF[kt % 2][:], xs2(kt, scb, scb + 512), AF.Square),
                            deps=[xb_toks[kt], sq_toks[kt % 2]], tick=True)
                        sq_toks[kt % 2] = mm(bkq, bkq.t[0:1, 0:512], OC[:],
                                             SQF[kt % 2][:], kt == 0, kt == 7,
                                             deps=[tk_s], tick=True)
                    R = [ROWS[0:1, i * 512:(i + 1) * 512] for i in range(5)]
                    tk0 = rd("dve", bkx, lambda e, R=R, b=bkx: e.tensor_scalar(
                        R[0], b.t[0:1, 0:512], -1.0 / D, None, ALU.mult))
                    tk_m2 = pl.op("dve", lambda e, R=R: e.tensor_tensor(
                        R[1], R[0], R[0], ALU.mult), force=[tk0], tick=True)
                    tk_var = rd("dve", bkq, lambda e, R=R, b=bkq: e.scalar_tensor_tensor(
                        R[2], b.t[0:1, 0:512], 1.0 / D, R[1], ALU.mult,
                        ALU.subtract), force=[tk_m2])
                    tk_ln = pl.op("act", lambda e, R=R: e.activation(
                        R[3], R[2], AF.Ln, bias=EPSC[0:1, 0:1], scale=1.0),
                        deps=[tk_var], tick=True)
                    tk_rsd = pl.op("act", lambda e, R=R: e.activation(
                        R[4], R[3], AF.Exp, scale=-0.5), deps=[tk_ln],
                        force=[tk_ln], tick=True)
                    pl.op("dve", lambda e, R=R: e.tensor_copy(ROWB[0:1, 0:512], R[0]),
                          deps=[tk0], force=[tk0])
                    tk_rb = pl.op("dve", lambda e, R=R: e.tensor_copy(
                        ROWB[0:1, 512:1024], R[4]), deps=[tk_rsd], tick=True)
                    mm(bkx, bkx.t[:], OR[:], ROWB[0:1, 0:512], True, True,
                       deps=[tk_rb])
                    mm(bkq, bkq.t[:], OR[:], ROWB[0:1, 512:1024], True, True,
                       deps=[tk_rb])
                    for kt in range(NKT):
                        t1 = rd("dve", bkx, lambda e, kt=kt, scb=scb, b=bkx:
                                e.tensor_tensor(
                            TN[0][:], xs2(kt, scb, scb + 512), b.t[:], ALU.add),
                            tick=True)
                        rd("dve", bkq, lambda e, kt=kt, l=l, b=bkq:
                           e.scalar_tensor_tensor(
                            TN[1][:], b.t[:], LNG[:, kt:kt + 1],
                            TN[0][:], ALU.mult, ALU.mult))
                        xn_toks.append(pl.op("dve", lambda e, kt=kt, scb=scb, l=l:
                                             e.tensor_scalar(
                            XN2[kt][:, scb:scb + 512], TN[1][:],
                            LNB[:, kt:kt + 1], None, ALU.add),
                            tick=True))
                    xn_done_sc[sc] = pl.now("dve")

                pl.phase = "ffn.w1"
                gat_sc = [None, None]
                for j in range(NJ):
                    wstate["w1"][j % 4] = dma(
                        W1S[j % 4][:], w1ag[l][j],
                        deps=[wstate["w1"][j % 4]], q="pool")
                    tok_w1 = wstate["w1"][j % 4]
                    for sc in range(2):
                        scb = sc * 512
                        ba, bg = PS[2 * sc], PS[2 * sc + 1]
                        for kt in range(NKT):
                            mm(ba, ba.t[:], W1S[j % 4][:, kt * 256:kt * 256 + 128],
                               XN2[kt][:, scb:scb + 512], kt == 0, kt == 7,
                               deps=[tok_w1, xn_done_sc[sc]])
                        for kt in range(NKT):
                            mm(bg, bg.t[:], W1S[j % 4][:, kt * 256 + 128:kt * 256 + 256],
                               XN2[kt][:, scb:scb + 512], kt == 0, kt == 7,
                               deps=[tok_w1, xn_done_sc[sc]], tick=(kt == 7))
                        tk_gg = rd("act", bg, lambda e, b=bg, j=j, l=l: e.activation(
                            GG[:], b.t[:], AF.Gelu,
                            bias=B1G[:, j:j + 1]))
                        gat_sc[sc] = rd("dve", ba, lambda e, b=ba, j=j, sc=sc, l=l:
                           e.scalar_tensor_tensor(
                            GAT[:, j * 1024 + sc * 512:j * 1024 + sc * 512 + 512],
                            b.t[:], B1A[:, j:j + 1], GG[:],
                            ALU.add, ALU.mult), deps=[tk_gg])
                    wstate["w1"][j % 4] = pl.now("pe")
                if "ffndbg" in taps and l == 0 and cc == 0:
                    tpr = nc.dram_tensor("tap_rows", [1, 5 * 512], F32,
                                         kind="ExternalOutput").ap()
                    dma(tpr, ROWS[0:1, 0:5 * 512], deps=[pl.now("dve"), pl.now("act")])
                    tpx = nc.dram_tensor("tap_xn2", [NKT, 128, 1024], BF16,
                                         kind="ExternalOutput").ap()
                    tpg = nc.dram_tensor("tap_gat", [128, NJ * 1024], BF16,
                                         kind="ExternalOutput").ap()
                    deps = [pl.now(s) for s in ("pe", "dve", "act")]
                    for d in range(NKT):
                        dma(tpx[d], XN2[d][:], deps=deps)
                    dma(tpg, GAT[:], deps=deps)

                pl.phase = "ffn.w2"
                for m in range(8):
                    wstate["w2"][m % 2] = dma(
                        W2S[m % 2][:], w2p[l][m],
                        deps=[wstate["w2"][m % 2]], q="pool")
                    tok_w2 = wstate["w2"][m % 2]
                    for sc in range(2):
                        bk = PS[(2 * m + sc) % 4]
                        for j in range(NJ):
                            mm(bk, bk.t[:], W2S[m % 2][:, j * 128:(j + 1) * 128],
                               GAT[:, j * 1024 + sc * 512:j * 1024 + sc * 512 + 512],
                               j == 0, j == NJ - 1, deps=[tok_w2, gat_sc[sc]])
                        rd("dve", bk, lambda e, b=bk, m=m, fb=fb, sc=sc, l=l:
                           e.scalar_tensor_tensor(
                            X[m][:, fb + sc * 512 + 1:fb + sc * 512 + 513], b.t[:],
                            B2C[:, m:m + 1],
                            X[m][:, fb + sc * 512 + 1:fb + sc * 512 + 513],
                            ALU.add, ALU.add), deps=stash)
                    wstate["w2"][m % 2] = pl.now("pe")

    def mingru(silu_toks):
        with contextlib.ExitStack() as ph:
            EM = ph.enter_context
            pl.phase = "gru.xnb"
            pl.fence()
            H = EM(nc.sbuf_tensor("hbuf", [128, NET * S], BF16))
            with contextlib.ExitStack() as ph2:
                E2 = ph2.enter_context
                XNB = [E2(nc.sbuf_tensor(f"xnb{d}", [128, S], BF16)) for d in range(NKT)]
                XTM = [E2(nc.sbuf_tensor(f"mxt{i}", [128, D], F32)) for i in range(2)]
                XNTB = E2(nc.sbuf_tensor("xntb", [128, D], BF16))
                xtm_used = [None, None]
                xntb_used = None
                for t in range(NTT):
                    tk_l = dma(XTM[t % 2][:], xin_t[t], deps=[xtm_used[t % 2]])
                    tk_n = pl.op("dve", lambda e, t=t: e.tensor_scalar(
                        XNTB[:], XTM[t % 2][:], NMU[:, t:t + 1], RSD[:, t:t + 1],
                        ALU.add, ALU.mult), deps=[tk_l, xntb_used], tick=True)
                    xtm_used[t % 2] = tk_n
                    for d in range(NKT):
                        reg = d % 2
                        xntb_used = transp(PTB, PTB.t[:, reg * 128:(reg + 1) * 128],
                                           XNTB[:, d * 128:(d + 1) * 128], IDB[:],
                                           deps=[tk_n], first=True, last=True)
                        rd("act", PTB, lambda e, d=d, t=t, reg=reg: e.copy(
                            XNB[d][:, t * 128:(t + 1) * 128],
                            PTB.t[:, reg * 128:(reg + 1) * 128]))
                xnb_done = pl.now("act")

                WHGS = [E2(nc.sbuf_tensor(f"whgs{i}", [128, 8 * 256], BF16))
                        for i in range(2)]
                CB = E2(nc.sbuf_tensor("cbuf", [128, 512], F32))
                ZB = E2(nc.sbuf_tensor("zbuf", [128, 512], F32))
                RB = E2(nc.sbuf_tensor("rbuf", [128, 512], F32))
                VB = E2(nc.sbuf_tensor("vbuf", [128, 512], F32))
                SCN = [E2(nc.sbuf_tensor(f"scn{i}", [128, 512], F32)) for i in range(2)]
                CAR = E2(nc.sbuf_tensor("car", [128, NET], F32))
                pl.phase = "gru.scan"
                for et in range(NET):
                    wstate["whg"][et % 2] = dma(
                        WHGS[et % 2][:], whgi[et],
                        deps=[wstate["whg"][et % 2]], q="pool")
                    tok_w = wstate["whg"][et % 2]
                    car_tok = None
                    for mc in range(4):
                        bh, bgt = PS[2 * (mc % 2)], PS[2 * (mc % 2) + 1]
                        prev_s = pl.now("dve")
                        for kt in range(NKT):
                            mm(bh, bh.t[:], WHGS[et % 2][:, kt * 256:kt * 256 + 128],
                               XNB[kt][:, mc * 512:(mc + 1) * 512], kt == 0, kt == 7,
                               deps=[tok_w, xnb_done])
                        for kt in range(NKT):
                            mm(bgt, bgt.t[:], WHGS[et % 2][:, kt * 256 + 128:kt * 256 + 256],
                               XNB[kt][:, mc * 512:(mc + 1) * 512], kt == 0, kt == 7,
                               deps=[tok_w, xnb_done])
                        tk_c = rd("act", bgt, lambda e, b=bgt: e.activation(
                            CB[:], b.t[:], AF.Sigmoid, scale=-1.0),
                            deps=[prev_s])
                        tk_z = rd("act", bgt, lambda e, b=bgt: e.activation(
                            ZB[:], b.t[:], AF.Sigmoid), deps=[prev_s])
                        tk_r = rd("act", bh, lambda e, b=bh: e.activation(
                            RB[:], b.t[:], AF.Relu), deps=[prev_s])
                        tk_v = pl.op("dve", lambda e: e.scalar_tensor_tensor(
                            VB[:], RB[:], 0.5, ZB[:], ALU.add, ALU.mult),
                            deps=[tk_z, tk_r], tick=True)
                        init_ap = ZCOL[:] if mc == 0 else CAR[:, et:et + 1]
                        tk_s = pl.op("dve", lambda e, i=init_ap, mc=mc:
                                     e.tensor_tensor_scan(
                            SCN[mc % 2][:], CB[:], VB[:], i, ALU.mult, ALU.add),
                            deps=[tk_c, tk_v, car_tok], tick=True)
                        car_tok = pl.op("act", lambda e, et=et, mc=mc: e.copy(
                            CAR[:, et:et + 1], SCN[mc % 2][:, 511:512]),
                            deps=[tk_s], tick=True)
                        pl.op("dve", lambda e, et=et, mc=mc: e.tensor_copy(
                            H[:, et * S + mc * 512:et * S + (mc + 1) * 512],
                            SCN[mc % 2][:]), tick=True)
                    wstate["whg"][et % 2] = pl.now("pe")
                h_done = pl.now("dve")

            with contextlib.ExitStack() as ph3:
                E3 = ph3.enter_context
                pl.phase = "combine"
                pl.fence()
                XTM2 = [E3(nc.sbuf_tensor(f"cxt{i}", [128, D], F32)) for i in range(2)]
                XFM = [E3(nc.sbuf_tensor(f"xfm{d}", [128, 512], F32)) for d in range(NKT)]
                OTM = E3(nc.sbuf_tensor("otm", [128, 4 * 1024], F32))
                WOS = [E3(nc.sbuf_tensor(f"wos{i}", [128, NET * 128], BF16))
                       for i in range(2)]
                OFM = [E3(nc.sbuf_tensor(f"ofm{i}", [128, 512], F32)) for i in range(2)]
                xt_used = [None, None]
                otm_used = None
                xfm_used = None
                ofm_used = [None, None]
                for mc in range(4):
                    xfm_toks = []
                    for i in range(4):
                        t = mc * 4 + i
                        tk_l = dma(XTM2[i % 2][:], xin_t[t], deps=[xt_used[i % 2]])
                        for d in range(NKT):
                            bk = PS[2 + (d % 2)]
                            transp(bk, bk.t[:, 0:128],
                                   XTM2[i % 2][:, d * 128:(d + 1) * 128], IDF[:],
                                   deps=[tk_l])
                            xfm_toks.append(rd("act", bk, lambda e, d=d, i=i, b=bk:
                                               e.copy(XFM[d][:, i * 128:(i + 1) * 128],
                                                      b.t[:, 0:128]),
                                               deps=[xfm_used]))
                        xt_used[i % 2] = pl.now("pe")
                    pend_o = {}

                    def comb_trans(m):
                        tk_o = pend_o[m]
                        for tb in range(4):
                            bt = PS[2 + (tb % 2)]
                            ofm_used[m % 2] = transp(
                                bt, bt.t[:, 0:128],
                                OFM[m % 2][:, tb * 128:(tb + 1) * 128], IDF[:],
                                deps=[tk_o])
                            rd("act", bt, lambda e, b=bt, tb=tb, m=m: e.copy(
                                OTM[:, tb * 1024 + m * 128:tb * 1024 + (m + 1) * 128],
                                b.t[:, 0:128]), deps=[otm_used])

                    for m in range(8):
                        wstate["wos"][m % 2] = dma(
                            WOS[m % 2][:], wout[m],
                            deps=[wstate["wos"][m % 2]], q="pool")
                        tok_w = wstate["wos"][m % 2]
                        bk = PS[m % 2]
                        for kt in range(NET):
                            mm(bk, bk.t[:], WOS[m % 2][:, kt * 128:(kt + 1) * 128],
                               H[:, kt * S + mc * 512:kt * S + (mc + 1) * 512],
                               kt == 0, kt == NET - 1, deps=[tok_w, h_done])
                        tk_t = rd("dve", bk, lambda e, b=bk, m=m, mc=mc:
                                  e.scalar_tensor_tensor(
                            OFM[m % 2][:], b.t[:], 1.0,
                            X[m][:, mc * 512 + 1:mc * 512 + 513],
                            ALU.mult, ALU.mult),
                            deps=silu_toks + [ofm_used[m % 2]])
                        pend_o[m] = pl.op("dve", lambda e, m=m: e.tensor_tensor(
                            OFM[m % 2][:], OFM[m % 2][:], XFM[m][:], ALU.add),
                            deps=xfm_toks, tick=True)
                        xfm_used = pend_o[m]
                        if m > 0:
                            comb_trans(m - 1)
                        wstate["wos"][m % 2] = pl.now("pe")
                    comb_trans(7)
                    act_now = pl.now("act")
                    st_toks = []
                    for i in range(4):
                        st_toks.append(dma(out_t[mc * 4 + i],
                                           OTM[:, i * 1024:(i + 1) * 1024],
                                           deps=[act_now]))
                    otm_used = st_toks[-1]
                    finish_deps.extend(st_toks)

    phase0()
    if stop_after == "p0":
        if "p0" in taps:
            tp = nc.dram_tensor("tap_p0", [NKT, 128, S + 1], F32,
                                kind="ExternalOutput").ap()
            deps = [pl.now(s) for s in ("pe", "dve", "act")]
            for d in range(NKT):
                finish_deps.append(dma(tp[d], X[d][:, 0:S + 1], deps=deps))
        _finish(nc, pl)
        return nc

    nlayers = DEPTH6
    if stop_after and stop_after.startswith("layer"):
        nlayers = int(stop_after[5:]) + 1
    half_layer = stop_after == "attn0"
    substop = None
    if stop_after and stop_after.startswith("attnsub:"):
        substop = stop_after.split(":")[1]
        half_layer = True
        nlayers = 1
    for l in range(nlayers):
        attn_phase(l, substop=substop)
        if half_layer and l == 0:
            break
        ffn_phase(l)

    if stop_after and (stop_after.startswith("layer") or half_layer):
        taps = set(taps)
        if "xstream" in taps:
            tp = nc.dram_tensor("tap_x", [NKT, 128, S + 1], F32,
                                kind="ExternalOutput").ap()
            deps = [pl.now(s) for s in ("pe", "dve", "act")]
            for d in range(NKT):
                finish_deps.append(dma(tp[d], X[d][:, 0:S + 1], deps=deps))
        _finish(nc, pl)
        return nc

    lyr_scope.close()
    dve_done = pl.now("dve")
    silu_toks = [pl.op("act", lambda e, d=d: e.activation(
        X[d][:, 1:S + 1], X[d][:, 1:S + 1], AF.Silu), deps=[dve_done], tick=True)
        for d in range(NKT)]
    mingru(silu_toks)
    _finish(nc, pl)
    return nc


def _finish(nc, pl):
    finals = [(s, pl.cnt[s]) for s in QSEMS["sp"] if pl.cnt[s] > 0]
    sems = pl.sems
    pl.ops["sp"].append(
        lambda e, finals=tuple(finals): [e.wait_ge(sems[s], c) for s, c in finals])
    with nc.Block() as block:
        block.tensor(lambda e: [f(e) for f in pl.ops["pe"]])
        block.vector(lambda e: [f(e) for f in pl.ops["dve"]])
        block.scalar(lambda e: [f(e) for f in pl.ops["act"]])
        block.gpsimd(lambda e: [f(e) for f in pl.ops["pool"]])
        block.sync(lambda e: [f(e) for f in pl.ops["sp"]])


def make_inmaps(inputs):
    x = np.asarray(inputs["L0_attn_x"])
    pw = prep_weights(inputs)
    hc = host_consts()
    shared = {
        "wqkv": np.ascontiguousarray(inputs["L0_ffn_Wqkv"]),
        "wvh_d": pw["wvh"], "wob_d": pw["wob"],
        "w1ag": pw["w1ag"], "w2p": pw["w2p"], "whgi": pw["whgi"],
        "wout": pw["wout"],
        "b1a_cols": pw["b1a_cols"], "b1g_cols": pw["b1g_cols"],
        "lng_cols": pw["lng_cols"], "lnb_cols": pw["lnb_cols"],
        "b2_cols": pw["b2_cols"], "qs_cols": pw["qs_cols"],
        "ks_cols": pw["ks_cols"],
    }
    shared.update(hc)
    in_maps = []
    for b in range(8):
        m = dict(shared)
        m["x"] = np.ascontiguousarray(x[b])
        in_maps.append(m)
    return in_maps


def kernel(**inputs):
    in_maps = make_inmaps(inputs)
    nc = build()
    res = run_bass_kernel_spmd(nc, in_maps, core_ids=list(range(8)))
    out = np.stack([np.asarray(r["out"], np.float32) for r in res.results], 0)
    return out


# version 17
# speedup vs baseline: 1.3006x; 1.0008x over previous
import contextlib
import numpy as np
import ml_dtypes
import concourse.bass as bass
import concourse.mybir as mybir
from concourse.bass_utils import run_bass_kernel_spmd

F32 = mybir.dt.float32
BF16 = mybir.dt.bfloat16
AF = mybir.ActivationFunctionType
ALU = mybir.AluOpType
AX = mybir.AxisListType
BF = ml_dtypes.bfloat16

D = 1024
S = 2048
NKT = 8
NTT = 16
DH = 64
DIM_INNER = 2730
PAD_INNER = 2816
NJ = 22
DEPTH6 = 6
E_GRU = 2048
NET = 16
NEG = -1.0e38

ENGINES = ("pe", "dve", "act", "pool", "sp")
ESEM = dict(pe="pe", dve="dve", act="act", pool="pool", sp="sp_")
NQ = 6
QSEMS = dict(pool=[f"wq{i}" for i in range(NQ)],
             sp=[f"xq{i}" for i in range(NQ)])
ALL_SEMS = ["pe", "dve", "act", "pool"] + QSEMS["pool"] + QSEMS["sp"]


class PL:

    def __init__(self, sems):
        self.phase = ""
        self.labels = {e: [] for e in ENGINES}
        self.ops = {e: [] for e in ENGINES}
        self.sems = sems
        self.cnt = {s: 0 for s in sems}
        self.hwm = {e: {s: 0 for s in sems} for e in ENGINES}
        self.pending = {e: [] for e in ENGINES}
        self.qrot = {q: 0 for q in QSEMS}

    def fence(self, gate_pe=False):
        toks = [self.now(s) for s in ALL_SEMS if s != "pool"]
        for e in ENGINES:
            if e == "pe" and not gate_pe:
                continue
            self.pending[e] = list(toks)

    def op(self, eng, fn, deps=(), tick=False, dma=False, force=()):
        own = ESEM[eng]
        qsems = QSEMS.get(eng, ())
        waits = []
        force = list(force)
        tname, amt = (None, 0)
        if dma:
            tname = qsems[self.qrot[eng] % NQ]
            self.qrot[eng] += 1
            amt = 16
            if self.cnt[tname] > 0:
                force.append((tname, self.cnt[tname]))
        elif tick:
            tname = own
            amt = 1
        for dd in force:
            if dd is None:
                continue
            s, c = dd
            assert c <= self.cnt[s], (f"future wait: {eng} {s} {c} > {self.cnt[s]}")
            if c > self.hwm[eng][s]:
                self.hwm[eng][s] = c
                waits.append((s, c))
        alldeps = list(deps) + self.pending[eng]
        self.pending[eng] = []
        for dd in alldeps:
            if dd is None:
                continue
            s, c = dd
            assert c <= self.cnt[s], (f"future wait: {eng} {s} {c} > {self.cnt[s]}")
            if s == own or (s in qsems and not dma):
                continue
            if c > self.hwm[eng][s]:
                self.hwm[eng][s] = c
                waits.append((s, c))
        tok = None
        if tname is not None:
            self.cnt[tname] += amt
            tok = (tname, self.cnt[tname])
        sems = self.sems

        def run(e, waits=tuple(waits), fn=fn, tname=tname, amt=amt):
            for s, c in waits:
                e.wait_ge(sems[s], c)
            ins = fn(e)
            if tname is not None:
                ins.then_inc(sems[tname], amt)

        self.ops[eng].append(run)
        self.labels[eng].append((self.phase, getattr(self, "phase_shape", None)))
        return tok

    def now(self, sem):
        return (sem, self.cnt[sem])


class Bank:
    def __init__(self, tensor):
        self.t = tensor
        self.w = None
        self.r = []


def host_consts():
    c = {}
    c["identb"] = np.eye(128, dtype=BF)
    c["identf"] = np.eye(128, dtype=np.float32)
    rot = np.zeros((128, 128), np.float32)
    for o in (0, 64):
        for i in range(32):
            rot[o + 32 + i, o + i] = -1.0
            rot[o + i, o + 32 + i] = 1.0
    c["rotp"] = rot.astype(BF)
    sel = np.zeros((8, 128, 16), np.float32)
    selt = np.zeros((8, 16, 128), np.float32)
    for t in range(8):
        for p in range(128):
            h = 2 * t + p // 64
            sel[t, p, h] = 1.0
            selt[t, h, p] = 1.0
    c["sel"] = sel.astype(BF)
    c["selt"] = selt.astype(BF)
    c["onescol"] = np.ones((128, 1), BF)
    c["onesrow"] = np.ones((1, 128), BF)
    i = np.arange(128)[:, None]
    w = np.arange(384)[None, :]
    c["mask_mid"] = np.where((w >= i) & (w <= i + 256), 0.0, NEG).astype(BF)
    w0 = np.arange(128)[None, :]
    c["mask_t0"] = np.where(w0 <= i, 0.0, NEG).astype(BF)
    w1 = np.arange(256)[None, :]
    c["mask_t1"] = np.where(w1 <= i + 128, 0.0, NEG).astype(BF)
    half = np.arange(0, DH, 2, dtype=np.float64)
    inv_freq = 1.0 / 10000.0 ** (half / DH)
    t = np.arange(S, dtype=np.float64)
    freqs = t[:, None] * inv_freq[None, :]
    freqs = np.concatenate([freqs, freqs], -1)
    base_scale = (half + 0.4 * DH) / (1.4 * DH)
    power = (t - S // 2) / (256 // 2)
    scale = base_scale[None, :] ** power[:, None]
    scale = np.concatenate([scale, scale], -1)
    cosf, sinf = np.cos(freqs), np.sin(freqs)
    for name, tab in [("cq", cosf * scale), ("sq", sinf * scale),
                      ("ck", cosf / scale), ("sk", sinf / scale)]:
        fm = tab.T.astype(BF)
        c[name] = np.concatenate([fm, fm], 0)
    return c


def prep_weights(inputs):
    d = {}
    W1 = np.asarray(inputs["L0_ffn_W1"])
    a = np.zeros((DEPTH6, D, PAD_INNER), np.float32)
    g = np.zeros((DEPTH6, D, PAD_INNER), np.float32)
    a[:, :, :DIM_INNER] = W1[:, :, :DIM_INNER]
    g[:, :, :DIM_INNER] = W1[:, :, DIM_INNER:]
    w1ag = np.empty((DEPTH6, D, NJ * 2, 128), np.float32)
    w1ag[:, :, 0::2, :] = a.reshape(DEPTH6, D, NJ, 128)
    w1ag[:, :, 1::2, :] = g.reshape(DEPTH6, D, NJ, 128)
    w1v = w1ag.reshape(DEPTH6, NKT, 128, NJ, 256)
    d["w1ag"] = np.ascontiguousarray(w1v.transpose(0, 3, 2, 1, 4).reshape(
        DEPTH6, NJ, 128, NKT * 256))
    W2 = np.asarray(inputs["L0_ffn_W2"])
    w2p = np.zeros((DEPTH6, PAD_INNER, D), np.float32)
    w2p[:, :DIM_INNER] = W2
    w2v = w2p.reshape(DEPTH6, NJ, 128, 8, 128)
    d["w2p"] = np.ascontiguousarray(w2v.transpose(0, 3, 2, 1, 4).reshape(
        DEPTH6, 8, 128, NJ * 128))
    b1 = np.asarray(inputs["L0_ffn_b1"])
    b1a = np.zeros((DEPTH6, PAD_INNER), np.float32)
    b1g = np.zeros((DEPTH6, PAD_INNER), np.float32)
    b1a[:, :DIM_INNER] = b1[:, :DIM_INNER]
    b1g[:, :DIM_INNER] = b1[:, DIM_INNER:]
    d["b1a_cols"] = np.ascontiguousarray(
        b1a.reshape(DEPTH6, NJ, 128).transpose(0, 2, 1))
    d["b1g_cols"] = np.ascontiguousarray(
        b1g.reshape(DEPTH6, NJ, 128).transpose(0, 2, 1))
    for name, key in [("lng", "L0_ffn_ln_g"), ("lnb", "L0_ffn_ln_b"),
                      ("b2", "L0_ffn_b2")]:
        v = np.asarray(inputs[key]).reshape(DEPTH6, 8, 128)
        d[name + "_cols"] = np.ascontiguousarray(v.transpose(0, 2, 1))
    qs = np.asarray(inputs["L0_ffn_q_scale"])
    ks = np.asarray(inputs["L0_ffn_k_scale"])
    d["qs_cols"] = np.ascontiguousarray(np.concatenate([qs, qs], 1).T)
    d["ks_cols"] = np.ascontiguousarray(np.concatenate([ks, ks], 1).T)
    Whg = np.asarray(inputs["L0_attn_W_hg"])
    whgi = np.empty((D, NET, 256), np.float32)
    whgi[:, :, 0:128] = Whg[:, 0:E_GRU].reshape(D, NET, 128)
    whgi[:, :, 128:256] = Whg[:, E_GRU:].reshape(D, NET, 128)
    whv = whgi.reshape(NKT, 128, NET, 256)
    d["whgi"] = np.ascontiguousarray(whv.transpose(2, 1, 0, 3).reshape(
        NET, 128, NKT * 256))
    wo = np.asarray(inputs["L0_attn_W_out"])
    wov = wo.reshape(NET, 128, 8, 128)
    d["wout"] = np.ascontiguousarray(wov.transpose(2, 1, 0, 3).reshape(
        8, 128, NET * 128))
    wq = np.asarray(inputs["L0_ffn_Wqkv"])
    wv = wq[:, :, 2048:3072].reshape(DEPTH6, NKT, 128, 2, 512)
    d["wvh"] = np.ascontiguousarray(wv.transpose(0, 3, 2, 1, 4).reshape(
        DEPTH6, 2, 128, NKT * 512))
    wo2 = np.asarray(inputs["L0_ffn_Wo"]).reshape(DEPTH6, NKT, 128, 2, 512)
    d["wob"] = np.ascontiguousarray(wo2.transpose(0, 3, 2, 1, 4).reshape(
        DEPTH6, 2, 128, NKT * 512))
    return d


def build(stop_after=None, taps=()):
    nc = bass.Bass("TRN2", target_bir_lowering=False)
    taps = set(taps)

    x_in = nc.dram_tensor("x", [S, D], F32, kind="ExternalInput").ap()
    wqkv = nc.dram_tensor("wqkv", [DEPTH6, D, 3072], F32, kind="ExternalInput").ap()
    w1ag = nc.dram_tensor("w1ag", [DEPTH6, NJ, 128, NKT * 256], F32, kind="ExternalInput").ap()
    w2p = nc.dram_tensor("w2p", [DEPTH6, 8, 128, NJ * 128], F32, kind="ExternalInput").ap()
    whgi = nc.dram_tensor("whgi", [NET, 128, NKT * 256], F32, kind="ExternalInput").ap()
    wout = nc.dram_tensor("wout", [8, 128, NET * 128], F32, kind="ExternalInput").ap()
    b1a_c = nc.dram_tensor("b1a_cols", [DEPTH6, 128, NJ], F32, kind="ExternalInput").ap()
    b1g_c = nc.dram_tensor("b1g_cols", [DEPTH6, 128, NJ], F32, kind="ExternalInput").ap()
    lng_c = nc.dram_tensor("lng_cols", [DEPTH6, 128, 8], F32, kind="ExternalInput").ap()
    lnb_c = nc.dram_tensor("lnb_cols", [DEPTH6, 128, 8], F32, kind="ExternalInput").ap()
    b2_c = nc.dram_tensor("b2_cols", [DEPTH6, 128, 8], F32, kind="ExternalInput").ap()
    qs_c = nc.dram_tensor("qs_cols", [128, DEPTH6], F32, kind="ExternalInput").ap()
    ks_c = nc.dram_tensor("ks_cols", [128, DEPTH6], F32, kind="ExternalInput").ap()
    wvh_d = nc.dram_tensor("wvh_d", [DEPTH6, 2, 128, NKT * 512], F32, kind="ExternalInput").ap()
    wob_d = nc.dram_tensor("wob_d", [DEPTH6, 2, 128, NKT * 512], F32, kind="ExternalInput").ap()
    cst = {}
    for name, shape, dt in [
        ("identb", [128, 128], BF16), ("identf", [128, 128], F32),
        ("rotp", [128, 128], BF16), ("sel", [8, 128, 16], BF16),
        ("selt", [8, 16, 128], BF16), ("onescol", [128, 1], BF16),
        ("onesrow", [1, 128], BF16), ("mask_mid", [128, 384], BF16),
        ("mask_t0", [128, 128], BF16), ("mask_t1", [128, 256], BF16),
        ("cq", [128, S], BF16), ("sq", [128, S], BF16),
        ("ck", [128, S], BF16), ("sk", [128, S], BF16),
    ]:
        cst[name] = nc.dram_tensor(name, shape, dt, kind="ExternalInput").ap()
    out_d = nc.dram_tensor("out", [S, D], F32, kind="ExternalOutput").ap()
    out_t = out_d.rearrange("(t p) d -> t p d", p=128)
    xin_t = x_in.rearrange("(t p) d -> t p d", p=128)
    wqkv_v = wqkv.rearrange("l (kt p) e -> l p kt e", p=128)

    top = contextlib.ExitStack()
    E = top.enter_context
    sems = {s: E(nc.semaphore(s)) for s in ALL_SEMS}
    pl = PL(sems)

    X = [E(nc.sbuf_tensor(f"X{d}", [128, S + 4], F32)) for d in range(NKT)]
    IDB = E(nc.sbuf_tensor("s_identb", [128, 128], BF16))
    IDF = E(nc.sbuf_tensor("s_identf", [128, 128], F32))
    ROTP = E(nc.sbuf_tensor("s_rotp", [128, 128], BF16))
    SEL = [E(nc.sbuf_tensor(f"s_sel{t}", [128, 16], BF16)) for t in range(8)]
    SELT = [E(nc.sbuf_tensor(f"s_selt{t}", [16, 128], BF16)) for t in range(8)]
    OC = E(nc.sbuf_tensor("s_onescol", [128, 1], BF16))
    OR = E(nc.sbuf_tensor("s_onesrow", [1, 128], BF16))
    NMU = E(nc.sbuf_tensor("negmu", [128, NTT], F32))
    RSD = E(nc.sbuf_tensor("rstd", [128, NTT], F32))
    QS8 = E(nc.sbuf_tensor("qs8", [128, DEPTH6], F32))
    KS1 = E(nc.sbuf_tensor("ks1", [128, DEPTH6], F32))
    HSTA = E(nc.sbuf_tensor("hst_a", [128, 3 * NKT], F32))
    HSTF = E(nc.sbuf_tensor("hst_f", [128, NKT], F32))
    ZCOL = E(nc.sbuf_tensor("zcol", [128, 1], F32))
    EPSC = E(nc.sbuf_tensor("epsc", [128, 1], F32))
    lyr_scope = contextlib.ExitStack()
    EL_ = lyr_scope.enter_context
    MMK = EL_(nc.sbuf_tensor("maskm", [128, 384], BF16))
    TAB = {k: EL_(nc.sbuf_tensor("tab_" + k, [128, S], BF16))
           for k in ("cq", "sq", "ck", "sk")}

    PS = [Bank(E(nc.psum_tensor(f"ps{i}", [128, 512], F32))) for i in range(6)]
    PTB2 = [Bank(E(nc.psum_tensor(f"ptb{i}", [128, 768], BF16))) for i in range(2)]
    PTB = PTB2[0]
    PV0 = Bank(PS[0].t)
    PV1 = Bank(PS[1].t)

    def dma(dst_ap, src_ap, deps=(), q="sp"):
        return pl.op(q, lambda e, d=dst_ap, s=src_ap: e.dma_start(out=d, in_=s),
                     deps=deps, dma=True)

    def mm(bank, out_ap, lh, rh, start, stop, deps=(), tick=False):
        deps = list(deps)
        if start:
            deps += bank.r
            bank.r = []
        pl.phase_shape = f"{lh.shape[0]}*{lh.shape[-1]}*{rh.shape[-1]}"
        tok = pl.op("pe", lambda e, o=out_ap, a=lh, b=rh, st=start, sp=stop:
                    e.matmul(o, a, b, start=st, stop=sp), deps=deps,
                    tick=(tick or stop))
        pl.phase_shape = None
        if stop:
            bank.w = tok
        return tok

    def transp(bank, out_ap, in_ap, ident, deps=(), first=True, last=True):
        deps = list(deps)
        if first:
            deps += bank.r
            bank.r = []
        pl.phase_shape = "T"
        tok = pl.op("pe", lambda e, o=out_ap, i=in_ap, d=ident:
                    e.transpose(o, i, d), deps=deps, tick=True)
        pl.phase_shape = None
        if last:
            bank.w = tok
        return tok

    def rd(eng, bank, fn, deps=(), tick=True, force=()):
        tok = pl.op(eng, fn, deps=list(deps) + [bank.w], tick=tick, force=force)
        bank.r.append(tok)
        return tok

    for name, buf in [("mask_mid", MMK),
                      ("identb", IDB), ("identf", IDF), ("rotp", ROTP),
                      ("onescol", OC), ("onesrow", OR)]:
        dma(buf[:], cst[name])
    for k in TAB:
        dma(TAB[k][:], cst[k])
    for t in range(8):
        dma(SEL[t][:], cst["sel"][t])
        dma(SELT[t][:], cst["selt"][t])
    dma(QS8[:], qs_c)
    dma(KS1[:], ks_c)
    consts_toks = [pl.now(s) for s in QSEMS["sp"]]
    pl.op("dve", lambda e: e.tensor_scalar(QS8[:], QS8[:], 8.0, None, ALU.mult),
          deps=consts_toks, tick=True)
    pl.op("dve", lambda e: e.memset(ZCOL[:], 0.0), tick=True)
    pl.op("dve", lambda e: e.memset(EPSC[:], 1e-5), tick=True)

    finish_deps = []

    def phase0():
        with contextlib.ExitStack() as ph:
            E0 = ph.enter_context
            pl.phase = "p0"
            pl.fence()
            XT = [E0(nc.sbuf_tensor(f"xt{t}", [128, D], F32)) for t in range(NTT)]
            SQ0 = E0(nc.sbuf_tensor("sq0", [128, D], F32))
            CLS = E0(nc.sbuf_tensor("p0cols", [128, 8], F32))
            XNT = E0(nc.sbuf_tensor("xnt", [128, D], F32))
            tok_x = [dma(XT[t][:], xin_t[t]) for t in range(NTT)]
            stat_rs = []
            tk_ln = None
            tk_red = None
            for t in range(NTT):
                tk_r0 = pl.op("dve", lambda e, t=t: e.reduce_sum(
                    CLS[:, 0:1], XT[t][:], AX.X), deps=[tok_x[t]], tick=True)
                tk_sq = pl.op("act", lambda e, t=t: e.activation(
                    SQ0[:], XT[t][:], AF.Square), deps=[tok_x[t], tk_red], tick=True)
                tk_red = pl.op("dve", lambda e: e.reduce_sum(CLS[:, 1:2], SQ0[:], AX.X),
                               deps=[tk_sq], tick=True)
                pl.op("dve", lambda e, t=t: e.tensor_scalar(
                    NMU[:, t:t + 1], CLS[:, 0:1], -1.0 / D, None, ALU.mult),
                    force=[tk_r0])
                tk_m2 = pl.op("dve", lambda e: e.tensor_tensor(
                    CLS[:, 2:3], CLS[:, 0:1], CLS[:, 0:1], ALU.mult),
                    force=[tk_r0], tick=True)
                tk_m3 = pl.op("dve", lambda e: e.tensor_scalar(
                    CLS[:, 3:4], CLS[:, 2:3], 1.0 / (float(D) * D), None, ALU.mult),
                    force=[tk_m2], tick=True)
                tk_var = pl.op("dve", lambda e: e.scalar_tensor_tensor(
                    CLS[:, 4:5], CLS[:, 1:2], 1.0 / D, CLS[:, 3:4],
                    ALU.mult, ALU.subtract), deps=[tk_ln],
                    force=[tk_m3, tk_red], tick=True)
                tk_ln = pl.op("act", lambda e: e.activation(
                    CLS[:, 5:6], CLS[:, 4:5], AF.Ln, bias=EPSC[:], scale=1.0),
                    deps=[tk_var], tick=True)
                tk_rs = pl.op("act", lambda e, t=t: e.activation(
                    RSD[:, t:t + 1], CLS[:, 5:6], AF.Exp, scale=-0.5),
                    deps=[tk_ln], force=[tk_ln], tick=True)
                stat_rs.append(tk_rs)
            for d in range(NKT):
                pl.op("dve", lambda e, d=d: e.memset(X[d][:, 0:1], 0.0))
            xnt_used = None
            for t in range(NTT):
                tk_n = pl.op("dve", lambda e, t=t: e.tensor_scalar(
                    XNT[:], XT[t][:], NMU[:, t:t + 1], RSD[:, t:t + 1],
                    ALU.add, ALU.mult), deps=[stat_rs[t], xnt_used], tick=True)
                for d in range(NKT):
                    bk = PS[d % 2]
                    xnt_used = transp(bk, bk.t[:, 0:128],
                                      XNT[:, d * 128:(d + 1) * 128], IDF[:],
                                      deps=[tk_n])
                    rd("act", bk, lambda e, b=bk, t=t, d=d: e.copy(
                        X[d][:, 1 + t * 128: 1 + (t + 1) * 128], b.t[:, 0:128]))

    wstate = dict(wqk=None, wvh=[None, None], wob=[None, None],
                  w1=[None, None, None, None], w2=[None, None],
                  whg=[None, None], wos=[None, None])

    def attn_phase(l, substop=None):
        with contextlib.ExitStack() as ph:
            EA = ph.enter_context
            pl.fence()
            XB = [EA(nc.sbuf_tensor(f"axb{d}_l{l}", [128, 516], BF16)) for d in range(NKT)]
            WQK = EA(nc.sbuf_tensor(f"wqk_l{l}", [128, 8 * 2048], BF16))
            WVH = [EA(nc.sbuf_tensor(f"wvh{i}_l{l}", [128, 8 * 512], BF16))
                   for i in range(2)]
            WOB = [EA(nc.sbuf_tensor(f"wob{i}_l{l}", [128, 8 * 512], BF16))
                   for i in range(2)]
            QR = [EA(nc.sbuf_tensor(f"qr{m}_l{l}", [128, 512], BF16)) for m in range(8)]
            KR = [EA(nc.sbuf_tensor(f"kr{m}_l{l}", [128, 768], BF16)) for m in range(8)]
            VR = [EA(nc.sbuf_tensor(f"vrr{s}_l{l}", [128, 1024], BF16)) for s in range(6)]
            AC = [EA(nc.sbuf_tensor(f"acc{d}_l{l}", [128, 512], BF16)) for d in range(NKT)]
            SSF = EA(nc.sbuf_tensor(f"ssf_l{l}", [16, 512], F32))
            NIV = EA(nc.sbuf_tensor(f"niv_l{l}", [16, 512], BF16))
            PN = [EA(nc.sbuf_tensor(f"pn{i}_l{l}", [128, 768], BF16))
                  for i in range(2)]
            PTS = [EA(nc.sbuf_tensor(f"pts{i}_l{l}", [128, 768], BF16))
                   for i in range(2)]
            NML = EA(nc.sbuf_tensor(f"nml_l{l}", [128, 16], F32))
            RCL = EA(nc.sbuf_tensor(f"rcl_l{l}", [128, 4], F32))

            wqk_src = wqkv_v[l][:, :, 0:2048]
            wqk_dst = WQK[:].rearrange("p (kt e) -> p kt e", e=2048)
            wstate["wqk"] = dma(wqk_dst, wqk_src, deps=[wstate["wqk"]], q="pool")
            tok_wqk = wstate["wqk"]
            tok_wv = [None, None]
            for eh in range(2):
                tok_wv[eh] = dma(WVH[eh][:], wvh_d[l][eh],
                                 deps=[wstate["wvh"][eh]], q="pool")
            tok_wob = [None, None]
            for hf in range(2):
                tok_wob[hf] = dma(WOB[hf][:], wob_d[l][hf],
                                  deps=[wstate["wob"][hf]], q="pool")

            dve_now = pl.now("dve")
            stash = []
            for d in range(NKT):
                for i in range(3):
                    stash.append(pl.op("act", lambda e, d=d, i=i: e.copy(
                        HSTA[:, 3 * d + i:3 * d + i + 1],
                        X[d][:, (i + 1) * 512:(i + 1) * 512 + 1]),
                        deps=[dve_now], tick=True))

            for c in range(4):
                if c > 0:
                    pl.fence()
                base = c * 512
                xb_toks = []
                for d in range(NKT):
                    if c == 0:
                        pl.op("dve", lambda e, d=d: e.tensor_copy(
                            XB[d][:, 0:1], X[d][:, 0:1]))
                    else:
                        pl.op("dve", lambda e, d=d, c=c: e.tensor_copy(
                            XB[d][:, 0:1], HSTA[:, 3 * d + c - 1:3 * d + c]),
                            deps=[stash[3 * d + c - 1]])
                    xb_toks.append(pl.op("dve", lambda e, d=d, base=base: e.tensor_copy(
                        XB[d][:, 1:513], X[d][:, base + 1:base + 513]), tick=True))

                def xs_ap(kt, lo, hi):
                    off = 0 if kt >= 4 else 1
                    return XB[kt][:, lo + off:hi + off]

                if substop == "xb":
                    return

                if c > 0:
                    for m in range(8):
                        pl.op("dve", lambda e, m=m: e.tensor_copy(
                            KR[m][:, 0:256], KR[m][:, 512:768]), tick=(m == 7))
                    for s in range(2):
                        pl.op("dve", lambda e, s=s: e.tensor_copy(
                            VR[s][:], VR[4 + s][:]), tick=(s == 1))
                roll_tok = pl.now("dve")

                pl.phase = f"attn.prep"
                qsq_used = [None, None]
                perm_used = {}
                sq_done = {}
                PARTS = [(0, QS8, "cq", "sq"), (1024, KS1, "ck", "sk")]
                cp_toks = {}

                def raw_ap(part, m):
                    return QR[m][:, 0:512] if part == 0 else KR[m][:, 256:768]

                for part, (eoff, scl, tabc, tabs_) in enumerate(PARTS):
                    for m in range(8):
                        bk = PS[m % 4]
                        for kt in range(NKT):
                            mm(bk, bk.t[:], WQK[:, kt * 2048 + eoff + m * 128:
                                                 kt * 2048 + eoff + (m + 1) * 128],
                               xs_ap(kt, 0, 512), kt == 0, kt == 7,
                               deps=[tok_wqk, xb_toks[kt]])
                        cp_toks[(part, m)] = rd(
                            "dve", bk, lambda e, b=bk, a=raw_ap(part, m):
                            e.tensor_copy(a, b.t[:]),
                            deps=([roll_tok] if part == 1 else []))

                def sumsq_group(part):
                    for m in range(8):
                        tk_sq = pl.op("act", lambda e, a=raw_ap(part, m), m=m:
                                      e.activation(PN[m % 2][:, 0:512], a, AF.Square),
                                      deps=[cp_toks[(part, m)], qsq_used[m % 2]],
                                      tick=True)
                        sq_done[(part, m)] = tk_sq
                        qsq_used[m % 2] = mm(
                            PS[4], PS[4].t[0:16, 0:512], SEL[m][:],
                            PN[m % 2][:, 0:512], m == 0, m == 7, deps=[tk_sq],
                            tick=True)
                    tk_ssf = rd("dve", PS[4], lambda e: e.tensor_copy(
                        SSF[:], PS[4].t[0:16, 0:512]))
                    tk_mx = pl.op("dve", lambda e: e.tensor_scalar(
                        SSF[:], SSF[:], 1e-24, None, ALU.max), force=[tk_ssf],
                        tick=True)
                    tk_ln = pl.op("act", lambda e: e.activation(
                        SSF[:], SSF[:], AF.Ln), deps=[tk_mx], tick=True)
                    return pl.op("act", lambda e: e.activation(
                        NIV[:], SSF[:], AF.Exp, scale=-0.5), deps=[tk_ln],
                        force=[tk_ln], tick=True)

                def v_group_tt(eh, tt, tok_wv):
                    bk = PS[tt % 2]
                    for kt in range(NKT):
                        off = 0 if kt >= 4 else 1
                        mm(bk, bk.t[:], XB[kt][:, tt * 128 + off:tt * 128 + off + 128],
                           WVH[eh][:, kt * 512:(kt + 1) * 512], kt == 0, kt == 7,
                           deps=[tok_wv, xb_toks[kt], roll_tok])
                    rd("dve", bk, lambda e, b=bk, tt=tt, eh=eh: e.tensor_copy(
                        VR[2 + tt][:, eh * 512:(eh + 1) * 512], b.t[:]))

                def scale_rot(part, tk_niv, eh, tok_wv):
                    eoff, scl, tabc, tabs_ = PARTS[part]
                    for m in range(8):
                        a = raw_ap(part, m)
                        bb = PS[4 + (m % 2)] if part == 0 else PS[5 - (m % 2)]
                        mm(bb, bb.t[:], SELT[m][:], NIV[:], True, True,
                           deps=[tk_niv])
                        tk_scale = rd("dve", bb, lambda e, b=bb, a=a, scl=scl, l=l:
                                      e.scalar_tensor_tensor(
                            a, b.t[:], scl[:, l:l + 1], a, ALU.mult, ALU.mult),
                            deps=[sq_done[(part, m)]])
                        rb = PS[2 + (m % 2)]
                        tk_perm = mm(rb, rb.t[:], ROTP[:], a, True, True,
                                     deps=[tk_scale])
                        rd("dve", rb, lambda e, b=rb, tn=tabs_, base=base:
                           e.tensor_tensor(PTS[0][:, 0:512], b.t[:],
                                           TAB[tn][:, base:base + 512], ALU.mult))
                        pl.op("dve", lambda e, a=a, tn=tabc, base=base:
                              e.tensor_tensor(a, a,
                                              TAB[tn][:, base:base + 512], ALU.mult),
                              deps=[tk_perm])
                        pl.op("dve", lambda e, a=a: e.tensor_tensor(
                            a, a, PTS[0][:, 0:512], ALU.add), tick=True)
                        if m in (3, 5, 7):
                            v_group_tt(eh, (m - 1) // 2, tok_wv)

                tk_nivq = sumsq_group(0)
                v_group_tt(0, 0, tok_wv[0])
                scale_rot(0, tk_nivq, 0, tok_wv[0])
                tk_nivk = sumsq_group(1)
                v_group_tt(1, 0, tok_wv[1])
                scale_rot(1, tk_nivk, 1, tok_wv[1])
                if c == 3:
                    wstate["wvh"][0] = pl.now("pe")
                    wstate["wvh"][1] = pl.now("pe")
                qk_done = pl.now("dve")
                v_done = pl.now("dve")
                prep_pe = pl.now("pe")
                if substop == "v":
                    return

                pl.phase = "attn.units"
                units = [(hp, qi) for hp in range(8) for qi in range(4)]

                def unit_geom(qi):
                    if c == 0 and qi < 2:
                        w = 128 * (qi + 1)
                        rc0 = 256
                        mask = MMK[:, 384 - w:384]
                    else:
                        w = 384
                        rc0 = qi * 128
                        mask = MMK[:, 0:384]
                    njb = w // 128
                    return w, rc0, mask, njb, qi + (3 - njb)

                stash_u = {}

                def emit_front(u):
                    hp, qi = units[u]
                    p = u % 2
                    w, rc0, mask, njb, slot0 = unit_geom(qi)
                    sbk = [PS[2 + 2 * p], PS[3 + 2 * p]]
                    sim_toks = []
                    for hi, hb in ((0, 0), (1, 64)):
                        bk = sbk[hi]
                        mm(bk, bk.t[:, 0:w], IDB[:], mask, True, False)
                        sim_toks.append(mm(
                            bk, bk.t[:, 0:w],
                            QR[hp][hb:hb + 64, qi * 128:(qi + 1) * 128],
                            KR[hp][hb:hb + 64, rc0:rc0 + w], False, True,
                            deps=[qk_done]))
                    nm = []
                    for hi in range(2):
                        bk = sbk[hi]
                        nm.append(rd("dve", bk, lambda e, b=bk, hi=hi, w=w, p=p:
                                     e.reduce_max(NML[:, 8 * p + hi:8 * p + hi + 1],
                                                  b.t[:, 0:w], AX.X, negate=True)))
                    exps = []
                    for hi in range(2):
                        bk = sbk[hi]
                        exps.append(rd("act", bk, lambda e, b=bk, hi=hi, w=w, p=p:
                                       e.activation(
                            PN[p][:, hi * 384:hi * 384 + w], b.t[:, 0:w], AF.Exp,
                            bias=NML[:, 8 * p + hi:8 * p + hi + 1],
                            accum_out=NML[:, 8 * p + 2 + hi:8 * p + 3 + hi]),
                            deps=[nm[hi], stash_u.get(u - 2, {}).get("tr_last"),
                                  (prep_pe if u < 2 else None)]))
                    stash_u[u] = dict(exps=exps, njb=njb, w=w,
                                      slot0=slot0, hp=hp, qi=qi, p=p)

                def emit_scale(u):
                    st_ = stash_u[u]
                    p = st_["p"]
                    w = st_["w"]
                    exps = st_["exps"]
                    tk_rcl = pl.op("dve", lambda e, p=p: e.reciprocal(
                        RCL[:, 2 * p:2 * p + 2], NML[:, 8 * p + 2:8 * p + 4]),
                        deps=exps, tick=True)
                    sc_toks = []
                    for hi in range(2):
                        sc_toks.append(pl.op("act", lambda e, hi=hi, w=w, p=p:
                                             e.activation(
                            PN[p][:, hi * 384:hi * 384 + w],
                            PN[p][:, hi * 384:hi * 384 + w], AF.Copy,
                            scale=RCL[:, 2 * p + hi:2 * p + hi + 1]),
                            deps=[exps[hi], tk_rcl], tick=True))
                    st_["sc"] = sc_toks

                def emit_trans(u):
                    st_ = stash_u[u]
                    njb = st_["njb"]
                    p = u % 2
                    ptb = PTB2[p]
                    trs = []
                    first = True
                    for hi in range(2):
                        for jb in range(njb):
                            dep = [st_["sc"][hi]]
                            trs.append(transp(
                                ptb, ptb.t[:, (hi * 3 + jb) * 128:(hi * 3 + jb + 1) * 128],
                                PN[p][:, hi * 384 + jb * 128:hi * 384 + (jb + 1) * 128],
                                IDB[:], deps=dep, first=first,
                                last=(hi == 1 and jb == njb - 1)))
                            first = False
                    st_["tr_last"] = trs[-1]
                    pts_war = stash_u.get(u - 2, {}).get("pv_last")
                    if njb == 3:
                        e0 = rd("dve", ptb, lambda e, p=p: e.tensor_copy(
                            PTS[p][:, 0:768], ptb.t[:, 0:768]), deps=[pts_war])
                        st_["ptse"] = e0
                    else:
                        e0 = rd("dve", ptb, lambda e, njb=njb, p=p: e.tensor_copy(
                            PTS[p][:, 0:njb * 128], ptb.t[:, 0:njb * 128]),
                            deps=[pts_war])
                        st_["ptse"] = rd("dve", ptb, lambda e, njb=njb, p=p:
                                         e.tensor_copy(
                            PTS[p][:, 384:384 + njb * 128],
                            ptb.t[:, 384:384 + njb * 128]))
                    st_["ptse0"] = e0

                def emit_pv(u):
                    st_ = stash_u[u]
                    hp, qi, njb, slot0 = st_["hp"], st_["qi"], st_["njb"], st_["slot0"]
                    p = u % 2
                    pvb = [PV0, PV1][p]
                    for hi, hb in ((0, 0), (1, 64)):
                        for jb in range(njb):
                            st_["pv_last"] = mm(
                               pvb, PS[p].t[hb:hb + 64, 0:128],
                               VR[slot0 + jb][:, (2 * hp + hi) * 64:(2 * hp + hi + 1) * 64],
                               PTS[p][:, hi * 384 + jb * 128:hi * 384 + (jb + 1) * 128],
                               jb == 0, jb == njb - 1,
                               deps=[st_["ptse0"], st_["ptse"], v_done])
                    tk_a = pl.op("act", lambda e, hp=hp, qi=qi, p=p: e.copy(
                        AC[hp][:, qi * 128:(qi + 1) * 128], PS[p].t[:, 0:128]),
                        deps=[pvb.w], tick=True)
                    pvb.r.append(tk_a)
                    stash_u.pop(u - 2, None)

                for u in range(len(units)):
                    emit_front(u)
                    if u > 0:
                        emit_trans(u - 1)
                    if u > 1:
                        emit_pv(u - 2)
                    emit_scale(u)
                emit_trans(len(units) - 1)
                emit_pv(len(units) - 2)
                emit_pv(len(units) - 1)
                a_done = pl.now("act")
                if substop == "units":
                    return

                pl.phase = "attn.wo"
                res_toks = []
                for m in range(8):
                    bk = PS[m % 4]
                    for kt in range(NKT):
                        mm(bk, bk.t[:], WOB[m // 4][:, kt * 512 + (m % 4) * 128:
                                                    kt * 512 + (m % 4) * 128 + 128],
                           AC[kt][:, 0:512], kt == 0, kt == 7,
                           deps=[tok_wob[m // 4], a_done])
                    if m == 3 and c == 3:
                        wstate["wob"][0] = pl.now("pe")
                    if m == 7 and c == 3:
                        wstate["wob"][1] = pl.now("pe")
                    res_toks.append(rd("dve", bk, lambda e, b=bk, m=m, base=base:
                                       e.scalar_tensor_tensor(
                        X[m][:, base + 1:base + 513], b.t[:], 1.0,
                        X[m][:, base + 1:base + 513], ALU.mult, ALU.add),
                        deps=stash))
            wstate["wqk"] = pl.now("pe")

    def ffn_phase(l):
        with contextlib.ExitStack() as ph:
            EF = ph.enter_context
            pl.fence()
            XB2 = [EF(nc.sbuf_tensor(f"fxb{d}_l{l}", [128, 1028], BF16)) for d in range(NKT)]
            GAT = EF(nc.sbuf_tensor(f"gated_l{l}", [128, NJ * 1024], BF16))
            SQF = [EF(nc.sbuf_tensor(f"sqf{i}_l{l}", [128, 512], BF16)) for i in range(2)]
            XN2 = [EF(nc.sbuf_tensor(f"xn2_{d}_l{l}", [128, 1024], BF16)) for d in range(NKT)]
            W1S = [EF(nc.sbuf_tensor(f"w1s{i}_l{l}", [128, 8 * 256], BF16)) for i in range(4)]
            W2S = [EF(nc.sbuf_tensor(f"w2s{i}_l{l}", [128, NJ * 128], BF16)) for i in range(2)]
            ROWS = EF(nc.sbuf_tensor(f"rows_l{l}", [1, 5 * 512], F32))
            ROWB = EF(nc.sbuf_tensor(f"rowb_l{l}", [1, 2 * 512], BF16))
            TN = [EF(nc.sbuf_tensor(f"tn{i}_l{l}", [128, 512], F32)) for i in range(2)]
            GG = EF(nc.sbuf_tensor(f"gg_l{l}", [128, 512], BF16))
            B1A = EF(nc.sbuf_tensor(f"b1a_l{l}", [128, NJ], F32))
            B1G = EF(nc.sbuf_tensor(f"b1g_l{l}", [128, NJ], F32))
            LNG = EF(nc.sbuf_tensor(f"lng_l{l}", [128, 8], F32))
            LNB = EF(nc.sbuf_tensor(f"lnb_l{l}", [128, 8], F32))
            B2C = EF(nc.sbuf_tensor(f"b2c_l{l}", [128, 8], F32))
            vec_toks = [dma(B1A[:], b1a_c[l]), dma(B1G[:], b1g_c[l]),
                        dma(LNG[:], lng_c[l]), dma(LNB[:], lnb_c[l]),
                        dma(B2C[:], b2_c[l])]
            pl.pending["dve"].extend(vec_toks)
            pl.pending["act"].extend(vec_toks)

            dve_now = pl.now("dve")
            stash = [pl.op("act", lambda e, d=d: e.copy(
                HSTF[:, d:d + 1], X[d][:, 1024:1025]), deps=[dve_now], tick=True)
                for d in range(NKT)]
            for cc in range(2):
                pl.phase = "ffn.ln"
                if cc > 0:
                    pl.fence()
                fb = cc * 1024
                xb_toks = []
                for d in range(NKT):
                    if cc == 0:
                        pl.op("dve", lambda e, d=d: e.tensor_copy(
                            XB2[d][:, 0:1], X[d][:, 0:1]))
                    else:
                        pl.op("dve", lambda e, d=d: e.tensor_copy(
                            XB2[d][:, 0:1], HSTF[:, d:d + 1]), deps=[stash[d]])
                    xb_toks.append(pl.op("dve", lambda e, d=d, fb=fb: e.tensor_copy(
                        XB2[d][:, 1:1025], X[d][:, fb + 1:fb + 1025]), tick=True))

                def xs2(kt, lo, hi):
                    off = 0 if kt >= 4 else 1
                    return XB2[kt][:, lo + off:hi + off]

                xn_done_sc = [None, None]
                xn_toks = []
                for sc in range(2):
                    scb = sc * 512
                    sq_toks = [None, None]
                    bkx, bkq = PS[2], PS[3]
                    for kt in range(NKT):
                        mm(bkx, bkx.t[0:1, 0:512], OC[:], xs2(kt, scb, scb + 512),
                           kt == 0, kt == 7, deps=[xb_toks[kt]], tick=(kt == 7))
                    for kt in range(NKT):
                        tk_s = pl.op("act", lambda e, kt=kt, scb=scb: e.activation(
                            SQF[kt % 2][:], xs2(kt, scb, scb + 512), AF.Square),
                            deps=[xb_toks[kt], sq_toks[kt % 2]], tick=True)
                        sq_toks[kt % 2] = mm(bkq, bkq.t[0:1, 0:512], OC[:],
                                             SQF[kt % 2][:], kt == 0, kt == 7,
                                             deps=[tk_s], tick=True)
                    R = [ROWS[0:1, i * 512:(i + 1) * 512] for i in range(5)]
                    tk0 = rd("dve", bkx, lambda e, R=R, b=bkx: e.tensor_scalar(
                        R[0], b.t[0:1, 0:512], -1.0 / D, None, ALU.mult))
                    tk_m2 = pl.op("dve", lambda e, R=R: e.tensor_tensor(
                        R[1], R[0], R[0], ALU.mult), force=[tk0], tick=True)
                    tk_var = rd("dve", bkq, lambda e, R=R, b=bkq: e.scalar_tensor_tensor(
                        R[2], b.t[0:1, 0:512], 1.0 / D, R[1], ALU.mult,
                        ALU.subtract), force=[tk_m2])
                    tk_ln = pl.op("act", lambda e, R=R: e.activation(
                        R[3], R[2], AF.Ln, bias=EPSC[0:1, 0:1], scale=1.0),
                        deps=[tk_var], tick=True)
                    tk_rsd = pl.op("act", lambda e, R=R: e.activation(
                        R[4], R[3], AF.Exp, scale=-0.5), deps=[tk_ln],
                        force=[tk_ln], tick=True)
                    pl.op("dve", lambda e, R=R: e.tensor_copy(ROWB[0:1, 0:512], R[0]),
                          deps=[tk0], force=[tk0])
                    tk_rb = pl.op("dve", lambda e, R=R: e.tensor_copy(
                        ROWB[0:1, 512:1024], R[4]), deps=[tk_rsd], tick=True)
                    mm(bkx, bkx.t[:], OR[:], ROWB[0:1, 0:512], True, True,
                       deps=[tk_rb])
                    mm(bkq, bkq.t[:], OR[:], ROWB[0:1, 512:1024], True, True,
                       deps=[tk_rb])
                    for kt in range(NKT):
                        t1 = rd("dve", bkx, lambda e, kt=kt, scb=scb, b=bkx:
                                e.tensor_tensor(
                            TN[0][:], xs2(kt, scb, scb + 512), b.t[:], ALU.add),
                            tick=True)
                        rd("dve", bkq, lambda e, kt=kt, l=l, b=bkq:
                           e.scalar_tensor_tensor(
                            TN[1][:], b.t[:], LNG[:, kt:kt + 1],
                            TN[0][:], ALU.mult, ALU.mult))
                        xn_toks.append(pl.op("dve", lambda e, kt=kt, scb=scb, l=l:
                                             e.tensor_scalar(
                            XN2[kt][:, scb:scb + 512], TN[1][:],
                            LNB[:, kt:kt + 1], None, ALU.add),
                            tick=True))
                    xn_done_sc[sc] = pl.now("dve")

                pl.phase = "ffn.w1"
                gat_sc = [None, None]
                for j in range(NJ):
                    wstate["w1"][j % 4] = dma(
                        W1S[j % 4][:], w1ag[l][j],
                        deps=[wstate["w1"][j % 4]], q="pool")
                    tok_w1 = wstate["w1"][j % 4]
                    for sc in range(2):
                        scb = sc * 512
                        ba, bg = PS[2 * sc], PS[2 * sc + 1]
                        for kt in range(NKT):
                            mm(ba, ba.t[:], W1S[j % 4][:, kt * 256:kt * 256 + 128],
                               XN2[kt][:, scb:scb + 512], kt == 0, kt == 7,
                               deps=[tok_w1, xn_done_sc[sc]])
                        for kt in range(NKT):
                            mm(bg, bg.t[:], W1S[j % 4][:, kt * 256 + 128:kt * 256 + 256],
                               XN2[kt][:, scb:scb + 512], kt == 0, kt == 7,
                               deps=[tok_w1, xn_done_sc[sc]], tick=(kt == 7))
                        tk_gg = rd("act", bg, lambda e, b=bg, j=j, l=l: e.activation(
                            GG[:], b.t[:], AF.Gelu,
                            bias=B1G[:, j:j + 1]))
                        gat_sc[sc] = rd("dve", ba, lambda e, b=ba, j=j, sc=sc, l=l:
                           e.scalar_tensor_tensor(
                            GAT[:, j * 1024 + sc * 512:j * 1024 + sc * 512 + 512],
                            b.t[:], B1A[:, j:j + 1], GG[:],
                            ALU.add, ALU.mult), deps=[tk_gg])
                    wstate["w1"][j % 4] = pl.now("pe")
                if "ffndbg" in taps and l == 0 and cc == 0:
                    tpr = nc.dram_tensor("tap_rows", [1, 5 * 512], F32,
                                         kind="ExternalOutput").ap()
                    dma(tpr, ROWS[0:1, 0:5 * 512], deps=[pl.now("dve"), pl.now("act")])
                    tpx = nc.dram_tensor("tap_xn2", [NKT, 128, 1024], BF16,
                                         kind="ExternalOutput").ap()
                    tpg = nc.dram_tensor("tap_gat", [128, NJ * 1024], BF16,
                                         kind="ExternalOutput").ap()
                    deps = [pl.now(s) for s in ("pe", "dve", "act")]
                    for d in range(NKT):
                        dma(tpx[d], XN2[d][:], deps=deps)
                    dma(tpg, GAT[:], deps=deps)

                pl.phase = "ffn.w2"
                for m in range(8):
                    wstate["w2"][m % 2] = dma(
                        W2S[m % 2][:], w2p[l][m],
                        deps=[wstate["w2"][m % 2]], q="pool")
                    tok_w2 = wstate["w2"][m % 2]
                    for sc in range(2):
                        bk = PS[(2 * m + sc) % 4]
                        for j in range(NJ):
                            mm(bk, bk.t[:], W2S[m % 2][:, j * 128:(j + 1) * 128],
                               GAT[:, j * 1024 + sc * 512:j * 1024 + sc * 512 + 512],
                               j == 0, j == NJ - 1, deps=[tok_w2, gat_sc[sc]])
                        rd("dve", bk, lambda e, b=bk, m=m, fb=fb, sc=sc, l=l:
                           e.scalar_tensor_tensor(
                            X[m][:, fb + sc * 512 + 1:fb + sc * 512 + 513], b.t[:],
                            B2C[:, m:m + 1],
                            X[m][:, fb + sc * 512 + 1:fb + sc * 512 + 513],
                            ALU.add, ALU.add), deps=stash)
                    wstate["w2"][m % 2] = pl.now("pe")

    def mingru(silu_toks):
        with contextlib.ExitStack() as ph:
            EM = ph.enter_context
            pl.phase = "gru.xnb"
            pl.fence()
            H = EM(nc.sbuf_tensor("hbuf", [128, NET * S], BF16))
            with contextlib.ExitStack() as ph2:
                E2 = ph2.enter_context
                XNB = [E2(nc.sbuf_tensor(f"xnb{d}", [128, S], BF16)) for d in range(NKT)]
                XTM = [E2(nc.sbuf_tensor(f"mxt{i}", [128, D], F32)) for i in range(2)]
                XNTB = E2(nc.sbuf_tensor("xntb", [128, D], BF16))
                xtm_used = [None, None]
                xntb_used = None
                for t in range(NTT):
                    tk_l = dma(XTM[t % 2][:], xin_t[t], deps=[xtm_used[t % 2]])
                    tk_n = pl.op("dve", lambda e, t=t: e.tensor_scalar(
                        XNTB[:], XTM[t % 2][:], NMU[:, t:t + 1], RSD[:, t:t + 1],
                        ALU.add, ALU.mult), deps=[tk_l, xntb_used], tick=True)
                    xtm_used[t % 2] = tk_n
                    for d in range(NKT):
                        reg = d % 2
                        xntb_used = transp(PTB, PTB.t[:, reg * 128:(reg + 1) * 128],
                                           XNTB[:, d * 128:(d + 1) * 128], IDB[:],
                                           deps=[tk_n], first=True, last=True)
                        rd("act", PTB, lambda e, d=d, t=t, reg=reg: e.copy(
                            XNB[d][:, t * 128:(t + 1) * 128],
                            PTB.t[:, reg * 128:(reg + 1) * 128]))
                xnb_done = pl.now("act")

                WHGS = [E2(nc.sbuf_tensor(f"whgs{i}", [128, 8 * 256], BF16))
                        for i in range(2)]
                CB = E2(nc.sbuf_tensor("cbuf", [128, 512], F32))
                ZB = E2(nc.sbuf_tensor("zbuf", [128, 512], F32))
                RB = E2(nc.sbuf_tensor("rbuf", [128, 512], F32))
                VB = E2(nc.sbuf_tensor("vbuf", [128, 512], F32))
                SCN = [E2(nc.sbuf_tensor(f"scn{i}", [128, 512], F32)) for i in range(2)]
                CAR = E2(nc.sbuf_tensor("car", [128, NET], F32))
                pl.phase = "gru.scan"
                for et in range(NET):
                    wstate["whg"][et % 2] = dma(
                        WHGS[et % 2][:], whgi[et],
                        deps=[wstate["whg"][et % 2]], q="pool")
                    tok_w = wstate["whg"][et % 2]
                    car_tok = None
                    for mc in range(4):
                        bh, bgt = PS[2 * (mc % 2)], PS[2 * (mc % 2) + 1]
                        prev_s = pl.now("dve")
                        for kt in range(NKT):
                            mm(bh, bh.t[:], WHGS[et % 2][:, kt * 256:kt * 256 + 128],
                               XNB[kt][:, mc * 512:(mc + 1) * 512], kt == 0, kt == 7,
                               deps=[tok_w, xnb_done])
                        for kt in range(NKT):
                            mm(bgt, bgt.t[:], WHGS[et % 2][:, kt * 256 + 128:kt * 256 + 256],
                               XNB[kt][:, mc * 512:(mc + 1) * 512], kt == 0, kt == 7,
                               deps=[tok_w, xnb_done])
                        tk_c = rd("act", bgt, lambda e, b=bgt: e.activation(
                            CB[:], b.t[:], AF.Sigmoid, scale=-1.0),
                            deps=[prev_s])
                        tk_z = rd("act", bgt, lambda e, b=bgt: e.activation(
                            ZB[:], b.t[:], AF.Sigmoid), deps=[prev_s])
                        tk_r = rd("act", bh, lambda e, b=bh: e.activation(
                            RB[:], b.t[:], AF.Relu), deps=[prev_s])
                        tk_v = pl.op("dve", lambda e: e.scalar_tensor_tensor(
                            VB[:], RB[:], 0.5, ZB[:], ALU.add, ALU.mult),
                            deps=[tk_z, tk_r], tick=True)
                        init_ap = ZCOL[:] if mc == 0 else CAR[:, et:et + 1]
                        tk_s = pl.op("dve", lambda e, i=init_ap, mc=mc:
                                     e.tensor_tensor_scan(
                            SCN[mc % 2][:], CB[:], VB[:], i, ALU.mult, ALU.add),
                            deps=[tk_c, tk_v, car_tok], tick=True)
                        car_tok = pl.op("act", lambda e, et=et, mc=mc: e.copy(
                            CAR[:, et:et + 1], SCN[mc % 2][:, 511:512]),
                            deps=[tk_s], tick=True)
                        pl.op("dve", lambda e, et=et, mc=mc: e.tensor_copy(
                            H[:, et * S + mc * 512:et * S + (mc + 1) * 512],
                            SCN[mc % 2][:]), tick=True)
                    wstate["whg"][et % 2] = pl.now("pe")
                h_done = pl.now("dve")

            with contextlib.ExitStack() as ph3:
                E3 = ph3.enter_context
                pl.phase = "combine"
                pl.fence()
                XTM2 = [E3(nc.sbuf_tensor(f"cxt{i}", [128, D], F32)) for i in range(2)]
                XFM = [E3(nc.sbuf_tensor(f"xfm{d}", [128, 512], F32)) for d in range(NKT)]
                OTM = E3(nc.sbuf_tensor("otm", [128, 4 * 1024], F32))
                WOS = [E3(nc.sbuf_tensor(f"wos{i}", [128, NET * 128], BF16))
                       for i in range(2)]
                OFM = [E3(nc.sbuf_tensor(f"ofm{i}", [128, 512], F32)) for i in range(2)]
                xt_used = [None, None]
                otm_used = None
                xfm_used = None
                ofm_used = [None, None]
                for mc in range(4):
                    xfm_toks = []
                    for i in range(4):
                        t = mc * 4 + i
                        tk_l = dma(XTM2[i % 2][:], xin_t[t], deps=[xt_used[i % 2]])
                        for d in range(NKT):
                            bk = PS[2 + (d % 2)]
                            transp(bk, bk.t[:, 0:128],
                                   XTM2[i % 2][:, d * 128:(d + 1) * 128], IDF[:],
                                   deps=[tk_l])
                            xfm_toks.append(rd("act", bk, lambda e, d=d, i=i, b=bk:
                                               e.copy(XFM[d][:, i * 128:(i + 1) * 128],
                                                      b.t[:, 0:128]),
                                               deps=[xfm_used]))
                        xt_used[i % 2] = pl.now("pe")
                    pend_o = {}

                    def comb_trans(m):
                        tk_o = pend_o[m]
                        for tb in range(4):
                            bt = PS[2 + (tb % 2)]
                            ofm_used[m % 2] = transp(
                                bt, bt.t[:, 0:128],
                                OFM[m % 2][:, tb * 128:(tb + 1) * 128], IDF[:],
                                deps=[tk_o])
                            rd("act", bt, lambda e, b=bt, tb=tb, m=m: e.copy(
                                OTM[:, tb * 1024 + m * 128:tb * 1024 + (m + 1) * 128],
                                b.t[:, 0:128]), deps=[otm_used])

                    for m in range(8):
                        wstate["wos"][m % 2] = dma(
                            WOS[m % 2][:], wout[m],
                            deps=[wstate["wos"][m % 2]], q="pool")
                        tok_w = wstate["wos"][m % 2]
                        bk = PS[m % 2]
                        for kt in range(NET):
                            mm(bk, bk.t[:], WOS[m % 2][:, kt * 128:(kt + 1) * 128],
                               H[:, kt * S + mc * 512:kt * S + (mc + 1) * 512],
                               kt == 0, kt == NET - 1, deps=[tok_w, h_done])
                        tk_t = rd("dve", bk, lambda e, b=bk, m=m, mc=mc:
                                  e.scalar_tensor_tensor(
                            OFM[m % 2][:], b.t[:], 1.0,
                            X[m][:, mc * 512 + 1:mc * 512 + 513],
                            ALU.mult, ALU.mult),
                            deps=silu_toks + [ofm_used[m % 2]])
                        pend_o[m] = pl.op("dve", lambda e, m=m: e.tensor_tensor(
                            OFM[m % 2][:], OFM[m % 2][:], XFM[m][:], ALU.add),
                            deps=xfm_toks, tick=True)
                        xfm_used = pend_o[m]
                        if m > 0:
                            comb_trans(m - 1)
                        wstate["wos"][m % 2] = pl.now("pe")
                    comb_trans(7)
                    act_now = pl.now("act")
                    st_toks = []
                    for i in range(4):
                        st_toks.append(dma(out_t[mc * 4 + i],
                                           OTM[:, i * 1024:(i + 1) * 1024],
                                           deps=[act_now]))
                    otm_used = st_toks[-1]
                    finish_deps.extend(st_toks)

    phase0()
    if stop_after == "p0":
        if "p0" in taps:
            tp = nc.dram_tensor("tap_p0", [NKT, 128, S + 1], F32,
                                kind="ExternalOutput").ap()
            deps = [pl.now(s) for s in ("pe", "dve", "act")]
            for d in range(NKT):
                finish_deps.append(dma(tp[d], X[d][:, 0:S + 1], deps=deps))
        _finish(nc, pl)
        return nc

    nlayers = DEPTH6
    if stop_after and stop_after.startswith("layer"):
        nlayers = int(stop_after[5:]) + 1
    half_layer = stop_after == "attn0"
    substop = None
    if stop_after and stop_after.startswith("attnsub:"):
        substop = stop_after.split(":")[1]
        half_layer = True
        nlayers = 1
    for l in range(nlayers):
        attn_phase(l, substop=substop)
        if half_layer and l == 0:
            break
        ffn_phase(l)

    if stop_after and (stop_after.startswith("layer") or half_layer):
        taps = set(taps)
        if "xstream" in taps:
            tp = nc.dram_tensor("tap_x", [NKT, 128, S + 1], F32,
                                kind="ExternalOutput").ap()
            deps = [pl.now(s) for s in ("pe", "dve", "act")]
            for d in range(NKT):
                finish_deps.append(dma(tp[d], X[d][:, 0:S + 1], deps=deps))
        _finish(nc, pl)
        return nc

    lyr_scope.close()
    dve_done = pl.now("dve")
    silu_toks = [pl.op("act", lambda e, d=d: e.activation(
        X[d][:, 1:S + 1], X[d][:, 1:S + 1], AF.Silu), deps=[dve_done], tick=True)
        for d in range(NKT)]
    mingru(silu_toks)
    _finish(nc, pl)
    return nc


def _finish(nc, pl):
    finals = [(s, pl.cnt[s]) for s in QSEMS["sp"] if pl.cnt[s] > 0]
    sems = pl.sems
    pl.ops["sp"].append(
        lambda e, finals=tuple(finals): [e.wait_ge(sems[s], c) for s, c in finals])
    with nc.Block() as block:
        block.tensor(lambda e: [f(e) for f in pl.ops["pe"]])
        block.vector(lambda e: [f(e) for f in pl.ops["dve"]])
        block.scalar(lambda e: [f(e) for f in pl.ops["act"]])
        block.gpsimd(lambda e: [f(e) for f in pl.ops["pool"]])
        block.sync(lambda e: [f(e) for f in pl.ops["sp"]])


def make_inmaps(inputs):
    x = np.asarray(inputs["L0_attn_x"])
    pw = prep_weights(inputs)
    hc = host_consts()
    shared = {
        "wqkv": np.ascontiguousarray(inputs["L0_ffn_Wqkv"]),
        "wvh_d": pw["wvh"], "wob_d": pw["wob"],
        "w1ag": pw["w1ag"], "w2p": pw["w2p"], "whgi": pw["whgi"],
        "wout": pw["wout"],
        "b1a_cols": pw["b1a_cols"], "b1g_cols": pw["b1g_cols"],
        "lng_cols": pw["lng_cols"], "lnb_cols": pw["lnb_cols"],
        "b2_cols": pw["b2_cols"], "qs_cols": pw["qs_cols"],
        "ks_cols": pw["ks_cols"],
    }
    shared.update(hc)
    in_maps = []
    for b in range(8):
        m = dict(shared)
        m["x"] = np.ascontiguousarray(x[b])
        in_maps.append(m)
    return in_maps


def kernel(**inputs):
    in_maps = make_inmaps(inputs)
    nc = build()
    res = run_bass_kernel_spmd(nc, in_maps, core_ids=list(range(8)))
    out = np.stack([np.asarray(r["out"], np.float32) for r in res.results], 0)
    return out
